# version 11
# baseline (speedup 1.0000x reference)
import contextlib
import os
from collections import defaultdict

import numpy as np
import concourse.bass as bass
import concourse.mybir as mybir

F32 = mybir.dt.float32
BF16 = mybir.dt.bfloat16
I32 = mybir.dt.int32
AF = mybir.ActivationFunctionType
OP = mybir.AluOpType
AX = mybir.AxisListType

D, FF, TOK, NT, T = 1024, 4096, 2048, 4, 512
KC, MC, PMC, PKC = 8, 32, 8, 32
EPS = 1e-5
MAGIC = 1597463007.0


class Plan:
    def __init__(self, nc):
        self.nc = nc
        self.names = ["sync", "gpsimd", "vector", "scalar", "tensor"]
        self.ops = {e: [] for e in self.names}
        self.semcount = defaultdict(int)
        self.sems = {}
        self._es = contextlib.ExitStack()

    def sem(self, name):
        if name not in self.sems:
            self.sems[name] = self._es.enter_context(self.nc.semaphore(name))
        return self.sems[name]

    def op(self, engine, emit, waits=(), inc=None, inc_k=1):
        waits = tuple(w for w in waits if w is not None)
        tok = None
        if inc is not None:
            self.semcount[inc] += inc_k
            tok = (inc, self.semcount[inc])
        self.ops[engine].append((emit, waits, inc, inc_k))
        return tok

    def dma(self, engine, dst, src, sem, waits=(), slow=False):
        return self.op(engine,
                       lambda e, d=dst, s=src, sl=slow: e.dma_start(
                           out=d, in_=s, allow_slow_non_contiguous=sl),
                       waits=waits, inc=sem, inc_k=16)

    def emit_all(self, block):
        waited = {e: defaultdict(int) for e in self.names}

        def mk(engine_name):
            def _body(eng):
                for emit, waits, inc, inc_k in self.ops[engine_name]:
                    for (s, v) in waits:
                        if waited[engine_name][s] >= v:
                            continue
                        eng.wait_ge(self.sem(s), v)
                        waited[engine_name][s] = v
                    instr = emit(eng)
                    if inc is not None:
                        if instr is None:
                            instr = eng.nop()
                        instr.then_inc(self.sem(inc), inc_k)
            return _body

        block.sync(mk("sync"))
        block.gpsimd(mk("gpsimd"))
        block.vector(mk("vector"))
        block.scalar(mk("scalar"))
        block.tensor(mk("tensor"))


def build_nc():
    nc = bass.Bass("TRN2", target_bir_lowering=False, detect_race_conditions=False)

    xT = nc.dram_tensor("xT", [D, TOK], F32, kind="ExternalInput")
    wfcT = nc.dram_tensor("wfcT", [D, FF], F32, kind="ExternalInput")
    wprT = nc.dram_tensor("wprT", [FF, D], F32, kind="ExternalInput")
    waT = nc.dram_tensor("waT", [D, 32], F32, kind="ExternalInput")
    wiT = nc.dram_tensor("wiT", [D, 32], F32, kind="ExternalInput")
    woutbT = nc.dram_tensor("woutbT", [33, D], F32, kind="ExternalInput")
    afcTd = nc.dram_tensor("afcTd", [128, 128], F32, kind="ExternalInput")
    bfcTd = nc.dram_tensor("bfcTd", [16, FF], F32, kind="ExternalInput")
    aprTd = nc.dram_tensor("aprTd", [128, 512], F32, kind="ExternalInput")
    bprTd = nc.dram_tensor("bprTd", [16, D], F32, kind="ExternalInput")
    gamma8 = nc.dram_tensor("gamma8", [128, KC], F32, kind="ExternalInput")
    beta8 = nc.dram_tensor("beta8", [128, KC], F32, kind="ExternalInput")
    bfc32d = nc.dram_tensor("bfc32d", [128, MC], F32, kind="ExternalInput")
    bpr8d = nc.dram_tensor("bpr8d", [128, PMC], F32, kind="ExternalInput")
    babi2 = nc.dram_tensor("babi2", [32, 2], F32, kind="ExternalInput")
    sel8 = nc.dram_tensor("sel8", [1, 8], F32, kind="ExternalInput")
    shard_fc = nc.dram_tensor("shard_fc", [128, FF], F32, kind="ExternalInput")
    shard_pr = nc.dram_tensor("shard_pr", [512, D], F32, kind="ExternalInput")
    outT = nc.dram_tensor("outT", [D, TOK], F32, kind="ExternalOutput")
    DBG = os.environ.get("DBG") == "1"
    if DBG:
        dbg_scl = nc.dram_tensor("dbg_scl", [128, 5], F32, kind="ExternalOutput")
        dbg_w0 = nc.dram_tensor("dbg_w0", [128, FF], BF16, kind="ExternalOutput")
        dbg_w5 = nc.dram_tensor("dbg_w5", [128, FF], BF16, kind="ExternalOutput")
        dbg_wp = nc.dram_tensor("dbg_wp", [256, D], BF16, kind="ExternalOutput")
        dbg_ats = nc.dram_tensor("dbg_ats", [128, 512 + 128], BF16, kind="ExternalOutput")
        dbg_xn = nc.dram_tensor("dbg_xn", [128, KC * T], BF16, kind="ExternalOutput")
        dbg_g1 = nc.dram_tensor("dbg_g1", [128, 8 * T], BF16, kind="ExternalOutput")
        dbg_u = nc.dram_tensor("dbg_u", [16, 2 * T], BF16, kind="ExternalOutput")
        dbg_h16 = nc.dram_tensor("dbg_h16", [33, TOK], BF16, kind="ExternalOutput")
        dbg_sA = nc.dram_tensor("dbg_sA", [32, TOK], F32, kind="ExternalOutput")
        dbg_rawB = nc.dram_tensor("dbg_rawB", [32, TOK], F32, kind="ExternalOutput")
        dbg_rawA = nc.dram_tensor("dbg_rawA", [32, TOK], F32, kind="ExternalOutput")
    wpr_eff = nc.dram_tensor("wpr_eff", [FF, D], BF16)
    am_in = nc.dram_tensor("am_in", [2, 1], F32)
    am_out = nc.dram_tensor("am_out", [2, 1], F32)
    ag_in = nc.dram_tensor("ag_in", [2, 32], F32)
    ag_out = nc.dram_tensor("ag_out", [16, 32], F32)
    amT_d = nc.dram_tensor("amT_d", [2, 128], F32)
    hin_d = nc.dram_tensor("hin_d", [1, 32], F32)
    rows_d = [nc.dram_tensor(f"rows_d{t}", [1, 1536], F32) for t in range(NT)]

    es = contextlib.ExitStack()
    E = es.enter_context

    weff_fc = [E(nc.sbuf_tensor(f"weff_fc{k}", [128, FF], BF16)) for k in range(KC)]
    g1 = E(nc.sbuf_tensor("g1", [128, PKC * T], BF16))
    x16 = [E(nc.sbuf_tensor(f"x16_{i}", [128, KC * T], BF16)) for i in range(2)]
    wstr = [E(nc.sbuf_tensor(f"wstr{i}", [128, 512], F32)) for i in range(6)]
    wprs = [E(nc.sbuf_tensor(f"wprs{i}", [128, D], BF16)) for i in range(6)]
    ki = [E(nc.sbuf_tensor(f"ki{i}", [128, 512], I32)) for i in range(2)]
    scanA = E(nc.sbuf_tensor("scanA", [32, TOK], F32))
    scanB = E(nc.sbuf_tensor("scanB", [32, TOK], F32))
    h16 = E(nc.sbuf_tensor("h16", [33, TOK], BF16))
    base_g1 = nc.lookup_mloc(g1).addr
    rows = nc.alloc_sbuf_tensor_at("rows", [1, 2048], F32, offset=base_g1)
    rows16 = nc.alloc_sbuf_tensor_at("rows16", [1, 512], BF16, offset=base_g1 + 8192)
    t1row = nc.alloc_sbuf_tensor_at("t1row", [1, 512], F32, offset=base_g1 + 9216)
    yirow = nc.alloc_sbuf_tensor_at("yirow", [1, 512], I32, offset=base_g1 + 11264)
    bc16 = E(nc.sbuf_tensor("bc16", [128, T], BF16))
    r1b16 = E(nc.sbuf_tensor("r1b16", [128, T], BF16))
    m1r1b16 = E(nc.sbuf_tensor("m1r1b16", [128, T], BF16))
    xr16 = [E(nc.sbuf_tensor(f"xr16_{i}", [128, T], BF16)) for i in range(2)]
    sq16 = [E(nc.sbuf_tensor(f"sq16_{i}", [128, T], BF16)) for i in range(2)]
    xfin = [E(nc.sbuf_tensor(f"xfin{i}", [128, T], F32)) for i in range(2)]
    bfcT16 = E(nc.sbuf_tensor("bfcT16", [16, FF], BF16))
    apr16 = E(nc.sbuf_tensor("apr16", [16, FF], BF16))
    g_s = [E(nc.sbuf_tensor(f"g_s{i}", [128, T], F32)) for i in range(2)]
    wab16 = E(nc.sbuf_tensor("wab16", [128, KC * 64], BF16))
    woutb16 = E(nc.sbuf_tensor("woutb16", [33, D], BF16))
    afcTs = E(nc.sbuf_tensor("afcTs", [128, 128], BF16))
    aprTs = E(nc.sbuf_tensor("aprTs", [128, 512], BF16))
    afcTf = E(nc.sbuf_tensor("afcTf", [128, 128], F32))
    aprTf = E(nc.sbuf_tensor("aprTf", [128, 512], F32))
    ufc16 = E(nc.sbuf_tensor("ufc16", [16, T], BF16))
    upr16 = E(nc.sbuf_tensor("upr16", [16, T], BF16))
    bprT16 = E(nc.sbuf_tensor("bprT16", [16, D], BF16))
    gam_sb = E(nc.sbuf_tensor("gam_sb", [128, KC], F32))
    bet_sb = E(nc.sbuf_tensor("bet_sb", [128, KC], F32))
    bet16 = E(nc.sbuf_tensor("bet16", [128, KC], BF16))
    bfc_sb = E(nc.sbuf_tensor("bfc_sb", [128, MC], F32))
    bpr_sb = E(nc.sbuf_tensor("bpr_sb", [128, PMC], F32))
    bprh_sb = E(nc.sbuf_tensor("bprh_sb", [128, PMC], F32))
    babi_sb = E(nc.sbuf_tensor("babi_sb", [32, 2], F32))
    dcol2 = E(nc.sbuf_tensor("dcol2", [32, 2], F32))
    halfda = E(nc.sbuf_tensor("halfda", [32, 1], F32))
    cg16 = E(nc.sbuf_tensor("cg16", [1, 64], BF16))
    sel_sb = E(nc.sbuf_tensor("sel_sb", [1, 8], F32))
    ones16 = E(nc.sbuf_tensor("ones16", [128, 1], BF16))
    amax_sb = E(nc.sbuf_tensor("amax_sb", [128, 8], F32))
    amT_sb = E(nc.sbuf_tensor("amT_sb", [2, 128], F32))
    am2_sb = E(nc.sbuf_tensor("am2_sb", [2, 1], F32))
    amax_bc = E(nc.sbuf_tensor("amax_bc", [128, 2], F32))
    inv_bc = E(nc.sbuf_tensor("inv_bc", [128, 2], F32))
    scl_sb = E(nc.sbuf_tensor("scl_sb", [128, 5], F32))
    amscr = E(nc.sbuf_tensor("amscr", [128, 1], F32))
    flat_sb = E(nc.sbuf_tensor("flat_sb", [1, 512], F32))
    hin_row = E(nc.sbuf_tensor("hin_row", [1, 32], F32))
    hin_col = E(nc.sbuf_tensor("hin_col", [32, 1], F32))
    tmp_row = E(nc.sbuf_tensor("tmp_row", [1, 64], F32))

    base_a = nc.lookup_mloc(apr16).addr
    th_s = nc.alloc_sbuf_tensor_at("th_s", [128, T], F32, offset=base_a)
    p1_s = nc.alloc_sbuf_tensor_at("p1_s", [128, T], F32, offset=base_a + 2048)
    outst = [nc.alloc_sbuf_tensor_at(f"outst{i}", [128, T], F32, offset=base_a + 4096 + 2048 * i)
             for i in range(2)]

    pb = [E(nc.psum_tensor(f"pb{i}", [128, T], F32)) for i in range(8)]

    P = Plan(nc)
    dma = P.dma

    t_small = None
    for dst, src in [
        (gam_sb[:, :], gamma8[:, :]), (bet_sb[:, :], beta8[:, :]),
        (bfc_sb[:, :], bfc32d[:, :]), (bpr_sb[:, :], bpr8d[:, :]),
        (babi_sb[:, :], babi2[:, :]), (sel_sb[:, :], sel8[:, :]),
        (afcTf[:, :], afcTd[:, :]), (aprTf[:, :], aprTd[:, :]),
    ]:
        t_small = dma("sync", dst, src, "lsm")
    dma("gpsimd", wab16.ap().rearrange("p (k c) -> p k c", c=64)[:, :, 0:32],
        waT.ap().rearrange("(k p) c -> p k c", p=128), "gsm")
    dma("gpsimd", wab16.ap().rearrange("p (k c) -> p k c", c=64)[:, :, 32:64],
        wiT.ap().rearrange("(k p) c -> p k c", p=128), "gsm")
    dma("gpsimd", woutb16[:, :], woutbT[:, :], "gsm")
    dma("gpsimd", bfcT16[:, :], bfcTd[:, :], "gsm")
    t_gsm = dma("gpsimd", bprT16[:, :], bprTd[:, :], "gsm")
    t_ones = P.op("vector", lambda e: e.memset(ones16[:, :], 1.0), inc="vd")
    t_h16one = P.op("vector", lambda e: e.memset(h16[32:33, :], 1.0), inc="vd")
    t_bet16 = P.op("vector", lambda e: e.tensor_copy(bet16[:, :], bet_sb[:, :]),
                   waits=[t_small], inc="vd")
    t_bprh = P.op("vector", lambda e: e.tensor_scalar_mul(bprh_sb[:, :], bpr_sb[:, :], 0.5),
                  waits=[t_small], inc="vd")

    t_piece = []
    piece_srcs = ([shard_fc[:, 1024 * i:1024 * (i + 1)] for i in range(4)] +
                  [shard_pr[128 * j:128 * (j + 1), :] for j in range(4)])
    prev_red = [None, None]
    for idx, psrc in enumerate(piece_srcs):
        t_l0 = dma("sync", wstr[0][:, :], psrc[:, 0:512], "ws0", waits=[prev_red[0]])
        t_l1 = dma("sync", wstr[1][:, :], psrc[:, 512:1024], "ws1", waits=[prev_red[1]])
        t_r0 = P.op("vector",
                    lambda e, idx=idx: e.tensor_reduce(
                        amax_sb[:, idx:idx + 1], wstr[0][:, :], AX.X, OP.max,
                        apply_absolute_value=True),
                    waits=[t_l0], inc="vd")
        t_r1 = P.op("vector",
                    lambda e: e.tensor_reduce(
                        amscr[:, :], wstr[1][:, :], AX.X, OP.max,
                        apply_absolute_value=True),
                    waits=[t_l1], inc="vd")
        t_rc = P.op("vector",
                    lambda e, idx=idx: e.tensor_max(
                        amax_sb[:, idx:idx + 1], amax_sb[:, idx:idx + 1], amscr[:, :]),
                    waits=[t_r0, t_r1], inc="vd")
        prev_red[0] = t_rc
        prev_red[1] = t_rc
        t_piece.append(t_rc)
    t_c1 = P.op("vector", lambda e: e.tensor_max(amax_sb[:, 0:1], amax_sb[:, 0:1], amax_sb[:, 1:2]),
                waits=[t_piece[1]], inc="vd")
    t_c2 = P.op("vector", lambda e: e.tensor_max(amax_sb[:, 2:3], amax_sb[:, 2:3], amax_sb[:, 3:4]),
                waits=[t_piece[3]], inc="vd")
    t_c3 = P.op("vector", lambda e: e.tensor_max(amax_sb[:, 0:1], amax_sb[:, 0:1], amax_sb[:, 2:3]),
                waits=[t_c1, t_c2], inc="vd")
    t_c4 = P.op("vector", lambda e: e.tensor_max(amax_sb[:, 4:5], amax_sb[:, 4:5], amax_sb[:, 5:6]),
                waits=[t_piece[5]], inc="vd")
    t_c5 = P.op("vector", lambda e: e.tensor_max(amax_sb[:, 6:7], amax_sb[:, 6:7], amax_sb[:, 7:8]),
                waits=[t_piece[7]], inc="vd")
    t_c6 = P.op("vector", lambda e: e.tensor_max(amax_sb[:, 1:2], amax_sb[:, 4:5], amax_sb[:, 6:7]),
                waits=[t_c4, t_c5, t_c3], inc="vd")
    t_amw = dma("gpsimd", amT_d.ap().rearrange("c p -> p c"), amax_sb[:, 0:2], "gam",
                waits=[t_c6], slow=True)
    t_amr = dma("gpsimd", amT_sb[:, :], amT_d[:, :], "gam", waits=[t_amw])
    t_am2 = P.op("vector", lambda e: e.tensor_reduce(am2_sb[:, :], amT_sb[:, :], AX.X, OP.max),
                 waits=[t_amr], inc="vd")
    t_amb = dma("gpsimd", am_in[:, :], am2_sb[:, :], "gam", waits=[t_am2])
    t_ar = P.op("gpsimd", lambda e: e.collective_compute(
        "AllReduce", OP.max, replica_groups=[list(range(8))],
        ins=[am_in.ap().opt()], outs=[am_out.ap().opt()]),
        waits=[t_amb], inc="cc1")
    t_amg = dma("gpsimd", am2_sb[:, :], am_out[:, :], "gam", waits=[t_ar])
    t_abc = dma("gpsimd", amax_bc.ap(), am_out.ap().rearrange("a b -> b a").broadcast_to([128, 2]),
                "gam", waits=[t_ar, t_amg])

    t_da_mm = None
    for k in range(KC):
        t_da_mm = P.op("tensor",
                       lambda e, k=k: e.matmul(pb[5][0:32, 0:1],
                                               wab16[:, 64 * k:64 * k + 32], bet16[:, k:k + 1],
                                               start=(k == 0), stop=(k == KC - 1)),
                       waits=[t_gsm, t_bet16], inc="td")
    t_da = P.op("vector", lambda e: e.tensor_add(dcol2[:, 0:1], pb[5][0:32, 0:1], babi_sb[:, 0:1]),
                waits=[t_da_mm, t_small], inc="vd")
    t_db_mm = None
    for k in range(KC):
        t_db_mm = P.op("tensor",
                       lambda e, k=k: e.matmul(pb[5][0:32, 1:2],
                                               wab16[:, 64 * k + 32:64 * k + 64], bet16[:, k:k + 1],
                                               start=(k == 0), stop=(k == KC - 1)),
                       waits=[t_da], inc="td")
    t_db = P.op("vector", lambda e: e.tensor_add(dcol2[:, 1:2], pb[5][0:32, 1:2], babi_sb[:, 1:2]),
                waits=[t_db_mm], inc="vd")
    t_hda = P.op("vector", lambda e: e.tensor_scalar_mul(halfda[:, :], dcol2[:, 0:1], 0.5),
                 waits=[t_da], inc="vd")
    t_wg = None
    for k in range(KC):
        t_wg = P.op("vector",
                    lambda e, k=k: e.tensor_scalar_mul(wab16[:, 64 * k:64 * (k + 1)],
                                                       wab16[:, 64 * k:64 * (k + 1)],
                                                       gam_sb[:, k:k + 1]),
                    waits=[t_db_mm, t_small], inc="vd")
    t_cg_mm = None
    for k in range(KC):
        t_cg_mm = P.op("tensor",
                       lambda e, k=k: e.matmul(pb[5][0:1, 0:64], ones16[:, :],
                                               wab16[:, 64 * k:64 * (k + 1)],
                                               start=(k == 0), stop=(k == KC - 1)),
                       waits=[t_wg, t_ones, t_db], inc="td")
    t_cg = P.op("vector", lambda e: e.tensor_copy(cg16[:, :], pb[5][0:1, 0:64]),
                waits=[t_cg_mm], inc="vd")

    t_scl = t_sclh = t_afcs = t_aprs = None

    def emit_scl_ops():
        nonlocal t_scl, t_sclh, t_afcs, t_aprs
        t_rec = P.op("vector", lambda e: e.reciprocal(inv_bc[:, :], amax_bc[:, :]),
                     waits=[t_abc], inc="vd")
        t_q1 = P.op("vector", lambda e: e.tensor_scalar_mul(scl_sb[:, 0:1], inv_bc[:, 0:1], 127.0),
                    waits=[t_rec], inc="vd")
        t_q2 = P.op("vector", lambda e: e.tensor_scalar_mul(scl_sb[:, 1:2], amax_bc[:, 0:1], 1.0 / 127.0),
                    waits=[t_abc], inc="vd")
        t_q3 = P.op("vector", lambda e: e.tensor_scalar_mul(scl_sb[:, 2:3], inv_bc[:, 1:2], 127.0),
                    waits=[t_rec], inc="vd")
        t_scl = P.op("vector", lambda e: e.tensor_scalar_mul(scl_sb[:, 3:4], amax_bc[:, 1:2], 1.0 / 127.0),
                     waits=[t_q1, t_q2, t_q3], inc="vd")
        t_sclh = P.op("vector", lambda e: e.tensor_scalar_mul(scl_sb[:, 4:5], scl_sb[:, 3:4], 0.5),
                      waits=[t_scl], inc="vd")
        t_afcs = P.op("vector", lambda e: e.tensor_scalar(
            afcTs[:, :], afcTf[:, :], scl_sb[:, 0:1], 2.0, OP.mult, OP.mult),
            waits=[t_scl, t_small], inc="vd")
        t_aprs = P.op("vector", lambda e: e.tensor_scalar(
            aprTs[:, :], aprTf[:, :], scl_sb[:, 2:3], 2.0, OP.mult, OP.mult),
            waits=[t_afcs], inc="vd")

    wstr_free = [prev_red[0], prev_red[1], None, None, None, None]
    ki_free = [None, None]
    fc_group_tok = [None] * 4
    spill_tok = [None] * PKC
    wprs_free = [None, None]

    def quant_quarter(j, src_ap, dst_ap, invs_ap, extra_dst_waits=()):
        buf = j % 6
        kb = j % 2
        t_l = dma("sync", wstr[buf][:, :], src_ap, f"ws{buf}", waits=[wstr_free[buf]])
        t_k = P.op("vector",
                   lambda e, b=buf, kb=kb, ia=invs_ap: e.tensor_scalar_mul(
                       ki[kb][:, :], wstr[b][:, :], ia),
                   waits=[t_l, t_scl, ki_free[kb]], inc="vd")
        t_w = P.op("scalar",
                   lambda e, da=dst_ap, kb=kb: e.activation(
                       out=da, in_=ki[kb][:, :], func=AF.Identity),
                   waits=[t_k] + list(extra_dst_waits), inc="sd")
        wstr_free[buf] = t_k
        ki_free[kb] = t_w
        return t_w

    qplan = []
    for c4 in range(4):
        for k in range(KC):
            for q in range(2):
                col = slice(1024 * c4 + 512 * q, 1024 * c4 + 512 * (q + 1))
                qplan.append((wfcT[128 * k:128 * (k + 1), col], weff_fc[k][:, col],
                              0, None))
    for ch in range(PKC):
        for q in range(2):
            qplan.append((wprT[128 * ch:128 * (ch + 1), 512 * q:512 * (q + 1)],
                          wprs[ch % 2][:, 512 * q:512 * (q + 1)], 2, ch))
    qidx = [0]

    def drain_q(n):
        while n > 0 and qidx[0] < len(qplan):
            j = qidx[0]
            qidx[0] += 1
            n -= 1
            src, dst, sc, ch = qplan[j]
            extra = ()
            if ch is not None and j % 2 == 0 and wprs_free[ch % 2] is not None:
                extra = (wprs_free[ch % 2],)
            t_w = quant_quarter(j, src, dst, scl_sb[:, sc:sc + 1],
                                extra_dst_waits=extra)
            if ch is None and j % 16 == 15:
                fc_group_tok[j // 16] = t_w
            if ch is not None and j % 2 == 1:
                wb = ch % 2
                t_wr = dma("scalar", wpr_eff[128 * ch:128 * (ch + 1), :],
                           wprs[wb][:, :], f"sw{wb}", waits=[t_w])
                spill_tok[ch] = t_wr
                wprs_free[wb] = t_wr

    bankD = [None] * 8
    bankF = [None] * 8
    bankF[5] = t_cg
    x16_free = [None, None]
    sq_free = [None, None]
    xr_free = [None, None]
    alpha_toks = []
    t_x16 = {}
    rw_prev = [None]

    def x16_load(t, tb, extra=()):
        return dma("gpsimd",
                   x16[tb].ap().rearrange("p (k f) -> p k f", f=T),
                   xT[:, T * t:T * (t + 1)].rearrange("(k p) f -> p k f", p=128),
                   f"xa{tb}", waits=extra)

    t_x16[0] = x16_load(0, 0)
    t_x16[1] = x16_load(1, 1)

    def newton_rsqrt(src_slice, out_slice, waits):
        t0 = P.op("vector", lambda e: e.tensor_single_scalar(
            yirow[:, :], rows[0:1, src_slice].bitcast(I32), 1, OP.arith_shift_right),
            waits=waits, inc="vd")
        last = P.op("vector", lambda e: e.tensor_scalar(
            yirow[:, :], yirow[:, :], -1.0, MAGIC, OP.mult, OP.add),
            waits=[t0], inc="vd")
        for it in range(3):
            a = P.op("vector", lambda e: e.tensor_mul(t1row[:, :], yirow.ap().bitcast(F32),
                                                      yirow.ap().bitcast(F32)),
                     waits=[last], inc="vd")
            b = P.op("vector", lambda e: e.tensor_mul(t1row[:, :], t1row[:, :],
                                                      rows[0:1, src_slice]),
                     waits=[a], inc="vd")
            c = P.op("vector", lambda e: e.tensor_scalar(t1row[:, :], t1row[:, :], -0.5, 1.5,
                                                         OP.mult, OP.add),
                     waits=[b], inc="vd")
            last = P.op("vector", lambda e: e.tensor_mul(yirow[:, :].bitcast(F32),
                                                         yirow.ap().bitcast(F32), t1row[:, :]),
                        waits=[c], inc="vd")
        return P.op("vector", lambda e: e.tensor_copy(rows[0:1, out_slice],
                                                      yirow.ap().bitcast(F32)),
                    waits=[last], inc="vd")

    r1s, m1r1s, r12s, scs = (slice(0, 512), slice(512, 1024),
                             slice(1024, 1536), slice(1536, 2048))

    stats_toks = {}

    def tileA_stats(t):
        tb = t % 2
        bs, bq = (3, 4) if t % 2 == 0 else (0, 1)
        tok_x = t_x16[t]
        t_sum = None
        for k in range(KC):
            w = [tok_x, t_ones, bankF[bs]] if k == 0 else []
            t_sum = P.op("tensor",
                         lambda e, k=k, tb=tb, bs=bs: e.matmul(pb[bs][0:1, :], ones16[:, :],
                                                               x16[tb][:, T * k:T * (k + 1)],
                                                               start=(k == 0), stop=(k == KC - 1)),
                         waits=w, inc="td")
        t_sqmm = None
        for k in range(KC):
            sb = k % 2
            t_sq = P.op("scalar",
                        lambda e, k=k, tb=tb, sb=sb: e.activation(
                            out=sq16[sb][:, :], in_=x16[tb][:, T * k:T * (k + 1)], func=AF.Square),
                        waits=[tok_x, sq_free[sb]], inc="sd")
            w = [t_sq] + ([bankF[bq]] if k == 0 else [])
            t_sqmm = P.op("tensor",
                          lambda e, k=k, sb=sb, bq=bq: e.matmul(pb[bq][0:1, :], ones16[:, :],
                                                                sq16[sb][:, :],
                                                                start=(k == 0), stop=(k == KC - 1)),
                          waits=w, inc="td")
            sq_free[sb] = t_sqmm
        stats_toks[t] = (t_sum, t_sqmm, bs, bq)

    def tileA(t):
        tb = t % 2
        t_sum, t_sqmm, bs, bq = stats_toks[t]
        bankD[bs], bankD[bq] = t_sum, t_sqmm
        R = rows
        t_m1 = P.op("vector", lambda e, bs=bs: e.tensor_scalar_mul(R[0:1, scs], pb[bs][0:1, :], 1.0 / D),
                    waits=[t_sum, rw_prev[0]], inc="vd")
        t_va = P.op("vector", lambda e, bq=bq: e.tensor_scalar_mul(R[0:1, r1s], pb[bq][0:1, :], 1.0 / D),
                    waits=[t_sqmm], inc="vd")
        t_vb = P.op("vector", lambda e: e.tensor_mul(R[0:1, m1r1s], R[0:1, scs], R[0:1, scs]),
                    waits=[t_m1], inc="vd")
        t_v1 = P.op("vector", lambda e: e.tensor_sub(R[0:1, r1s], R[0:1, r1s], R[0:1, m1r1s]),
                    waits=[t_va, t_vb], inc="vd")
        bankF[bs], bankF[bq] = t_m1, t_va
        t_a1 = P.op("vector", lambda e: e.tensor_scalar_add(R[0:1, r12s], R[0:1, r1s], EPS),
                    waits=[t_v1], inc="vd")
        t_r1 = newton_rsqrt(r12s, r12s, [t_a1])
        t_q = P.op("vector", lambda e: e.tensor_mul(R[0:1, m1r1s], R[0:1, r12s], R[0:1, r12s]),
                   waits=[t_r1], inc="vd")
        t_q2_ = P.op("vector", lambda e: e.tensor_mul(R[0:1, m1r1s], R[0:1, m1r1s], R[0:1, r1s]),
                     waits=[t_q], inc="vd")
        t_q3_ = P.op("vector", lambda e: e.tensor_scalar_add(R[0:1, m1r1s], R[0:1, m1r1s], EPS),
                     waits=[t_q2_], inc="vd")
        t_mvr1 = P.op("vector", lambda e: e.tensor_copy(R[0:1, r1s], R[0:1, r12s]),
                      waits=[t_q3_], inc="vd")
        t_r2 = newton_rsqrt(m1r1s, r12s, [t_mvr1])
        t_R12 = P.op("vector", lambda e: e.tensor_mul(R[0:1, r12s], R[0:1, r12s], R[0:1, r1s]),
                     waits=[t_r2], inc="vd")
        t_m1r1 = P.op("vector", lambda e: e.tensor_mul(R[0:1, m1r1s], R[0:1, scs], R[0:1, r1s]),
                      waits=[t_R12], inc="vd")
        t_S = P.op("vector", lambda e: e.scalar_tensor_tensor(
            R[0:1, scs], R[0:1, scs], -1.0, R[0:1, r12s], OP.mult, OP.mult),
            waits=[t_m1r1], inc="vd")
        t_S16 = P.op("vector", lambda e: e.tensor_copy(rows16[0:1, :], R[0:1, scs]),
                     waits=[t_S], inc="vd")
        t_rw = dma("gpsimd", rows_d[t][0:1, :], R[0:1, 0:1536], "grw", waits=[t_S])
        rw_prev[0] = t_rw
        t_bc3 = dma("gpsimd", bc16.ap(), rows_d[t][0:1, 1024:1536].broadcast_to([128, 512]),
                    "gbc", waits=[t_rw] + alpha_toks[-1:])
        t_za = t_zb = None
        for k in range(KC):
            xb = k % 2
            t_xr = P.op("vector",
                        lambda e, k=k, tb=tb, xb=xb: e.tensor_mul(
                            xr16[xb][:, :], x16[tb][:, T * k:T * (k + 1)], bc16[:, :]),
                        waits=[t_bc3, xr_free[xb]], inc="vd")
            w = [t_xr, t_wg] + ([bankF[2]] if k == 0 else [])
            t_za = P.op("tensor",
                        lambda e, k=k, xb=xb: e.matmul(pb[2][0:32, :],
                                                       wab16[:, 64 * k:64 * k + 32],
                                                       xr16[xb][:, :],
                                                       start=(k == 0), stop=False),
                        waits=w, inc="td")
            w2 = [bankF[5]] if k == 0 else []
            t_zb = P.op("tensor",
                        lambda e, k=k, xb=xb: e.matmul(pb[5][0:32, :],
                                                       wab16[:, 64 * k + 32:64 * (k + 1)],
                                                       xr16[xb][:, :],
                                                       start=(k == 0), stop=False),
                        waits=w2, inc="td")
            xr_free[xb] = t_zb
        t_za = P.op("tensor", lambda e: e.matmul(pb[2][0:32, :], cg16[0:1, 0:32],
                                                 rows16[0:1, :], start=False, stop=True),
                    waits=[t_S16, t_cg], inc="td")
        t_zb = P.op("tensor", lambda e: e.matmul(pb[5][0:32, :], cg16[0:1, 32:64],
                                                 rows16[0:1, :], start=False, stop=True),
                    inc="td")
        bankD[2], bankD[5] = t_za, t_zb
        t_th = P.op("scalar", lambda e: e.activation(out=xfin[0][0:32, :], in_=pb[2][0:32, :],
                                                     func=AF.Tanh, scale=0.5,
                                                     bias=halfda[:, :]),
                    waits=[t_za, t_hda], inc="sd")
        t_al = P.op("vector",
                    lambda e, t=t: e.tensor_scalar(scanA[:, T * t:T * (t + 1)], xfin[0][0:32, :],
                                                   0.5, 0.5, OP.mult, OP.add),
                    waits=[t_th], inc="vd")
        t_b = P.op("scalar",
                   lambda e, t=t: e.activation(out=scanB[:, T * t:T * (t + 1)],
                                               in_=pb[5][0:32, :], func=AF.Identity,
                                               bias=dcol2[:, 1:2]),
                   waits=[t_zb, t_db], inc="sd")
        bankF[2], bankF[5] = t_al, t_b
        alpha_toks.append(t_b)
        x16_free[tb] = t_zb
        return t_al, t_b

    a_toks = []
    tileA_stats(0)
    for t in range(NT):
        if t + 1 < NT:
            tileA_stats(t + 1)
        a_toks.append(tileA(t))
        if t + 2 < NT:
            t_x16[t + 2] = x16_load(t + 2, t % 2, extra=[x16_free[t % 2]])
        if t == 1:
            emit_scl_ops()
            drain_q(14)
        elif t == 2:
            drain_q(22)
        elif t == 3:
            drain_q(28)

    t_all_a = a_toks[-1][0]
    t_all_b = a_toks[-1][1]
    t_raw = None
    if DBG:
        dma("gpsimd", dbg_rawB[:, :], scanB[:, :], "gdbg", waits=[t_all_a, t_all_b])
        t_raw = dma("gpsimd", dbg_rawA[:, :], scanA[:, :], "gdbg", waits=[t_all_a, t_all_b])
    t_hl = P.op("vector", lambda e: e.tensor_tensor_scan(
        scanB[:, :], scanA[:, :], scanB[:, :], 0.0, OP.mult, OP.add),
        waits=[t_all_a, t_all_b, t_raw], inc="vd")
    t_cp = P.op("vector", lambda e: e.tensor_tensor_scan(
        scanA[:, :], scanA[:, :], scanA[:, :], 1.0, OP.mult, OP.bypass),
        waits=[t_hl], inc="vd")
    t_sw1 = dma("gpsimd", ag_in[0:1, :].rearrange("o f -> f o"), scanA[:, TOK - 1:TOK],
                "gsc", waits=[t_cp], slow=True)
    t_sw2 = dma("gpsimd", ag_in[1:2, :].rearrange("o f -> f o"), scanB[:, TOK - 1:TOK],
                "gsc", waits=[t_hl], slow=True)
    t_ag = P.op("gpsimd", lambda e: e.collective_compute(
        "AllGather", OP.bypass, replica_groups=[list(range(8))],
        ins=[ag_in.ap().opt()], outs=[ag_out.ap().opt()]),
        waits=[t_sw1, t_sw2], inc="cc2")

    finals_of = {}
    prep_done = {}

    def prep(t):
        tb = t % 2
        prev_fin_dep = finals_of.get(t - 2, [])
        t_bca = dma("gpsimd", r1b16.ap(), rows_d[t][0:1, 0:512].broadcast_to([128, 512]),
                    "gb32", waits=[prep_done.get(t - 1), rw_prev[0]])
        t_bcb = dma("gpsimd", m1r1b16.ap(), rows_d[t][0:1, 512:1024].broadcast_to([128, 512]),
                    "gb32")
        t_pair = t_bcb
        t_xl = x16_load(t, tb, extra=[x16_free[tb], a_toks[-1][1]] + prev_fin_dep[-1:])
        t_xn = None
        for k in range(KC):
            P.op("vector",
                 lambda e, k=k, tb=tb: e.tensor_mul(
                     x16[tb][:, T * k:T * (k + 1)], x16[tb][:, T * k:T * (k + 1)],
                     r1b16[:, :]),
                 waits=[t_xl, t_pair], inc="vd")
            t_xn = P.op("vector",
                        lambda e, k=k, tb=tb: e.tensor_sub(
                            x16[tb][:, T * k:T * (k + 1)], x16[tb][:, T * k:T * (k + 1)],
                            m1r1b16[:, :]),
                        inc="vd")
        prep_done[t] = t_xn
        return t_xn


    def finalize():
        P.op("gpsimd", lambda e: None,
             waits=[(s, P.semcount[s]) for s in ("vd", "sd", "td") if P.semcount[s]],
             inc="done")
        with nc.Block() as block:
            P.emit_all(block)
        P._es.close()
        es.close()
        return nc

    KSTOP = int(os.environ.get("KSTOP", "9"))
    if KSTOP <= 1:
        return finalize()

    t_xn0 = prep(0)

    gelu_done = {}
    t_ufc16 = {}
    t_upr16 = {}
    ufc_free = [None]
    upr_free = [None]

    def fc_upre(t, t_xn):
        tb = t % 2
        t_u = None
        for k in range(KC):
            w = [t_xn, t_afcs] + ([ufc_free[0], bankF[6]] if k == 0 else [])
            t_u = P.op("tensor",
                       lambda e, k=k, tb=tb: e.matmul(
                           pb[6][0:16, :], afcTs[:, 16 * k:16 * (k + 1)],
                           x16[tb][:, T * k:T * (k + 1)],
                           start=(k == 0), stop=(k == KC - 1)),
                       waits=w, inc="td")
        t_ufc16[t] = P.op("scalar",
                          lambda e: e.activation(out=ufc16[:, :], in_=pb[6][0:16, :],
                                                 func=AF.Identity),
                          waits=[t_u], inc="sd")
        ufc_free[0] = t_ufc16[t]

    fc_pend = {}

    def fc_group(t, t_xn, c4, gate_tok):
        tb = t % 2
        pend_up = fc_pend.setdefault(t, [])

        def emit_up(m, t_g):
            t_up = P.op("tensor",
                        lambda e, m=m: e.matmul(
                            pb[7][0:16, :], aprTs[:, 16 * m:16 * (m + 1)],
                            g1[:, T * m:T * (m + 1)],
                            start=(m == 0), stop=(m == MC - 1)),
                        waits=[t_g, t_aprs] + ([upr_free[0], bankF[7]] if m == 0 else []),
                        inc="td")
            if m == MC - 1:
                t_upr16[t] = P.op("scalar",
                                  lambda e: e.activation(out=upr16[:, :],
                                                         in_=pb[7][0:16, :],
                                                         func=AF.Identity),
                                  waits=[t_up], inc="sd")
                upr_free[0] = t_upr16[t]

        for m in range(8 * c4, 8 * c4 + 8):
            bk = m % 2
            w = [t_xn, bankF[bk] or bankD[bk], gate_tok]
            t_mm = None
            for k in range(KC):
                t_mm = P.op("tensor",
                            lambda e, k=k, m=m, tb=tb, bk=bk: e.matmul(
                                pb[bk][:, :], weff_fc[k][:, 128 * m:128 * (m + 1)],
                                x16[tb][:, T * k:T * (k + 1)],
                                start=(k == 0), stop=False),
                            waits=w if k == 0 else [], inc="td")
            if m == MC - 1:
                x16_free[tb] = t_mm
            while len(pend_up) > 1:
                emit_up(*pend_up.pop(0))
            t_mm = P.op("tensor",
                        lambda e, m=m, bk=bk: e.matmul(
                            pb[bk][:, :], bfcT16[:, 128 * m:128 * (m + 1)],
                            ufc16[:, :], start=False, stop=True),
                        waits=[t_ufc16[t], t_gsm], inc="td")
            bankD[bk] = t_mm
            t_g = P.op("scalar",
                       lambda e, m=m, bk=bk, t=t: e.activation(
                           out=g1[:, T * m:T * (m + 1)], in_=pb[bk][:, :],
                           func=AF.Gelu, bias=bfc_sb[:, m:m + 1],
                           scale=scl_sb[:, 1:2]),
                       waits=[t_mm, t_small], inc="sd")
            bankF[bk] = t_g
            gelu_done[(t, m)] = t_g
            pend_up.append((m, t_g))
        if c4 == 3:
            while pend_up:
                emit_up(*pend_up.pop(0))

    def fc(t, t_xn):
        fc_upre(t, t_xn)
        for c4 in range(4):
            fc_group(t, t_xn, c4, fc_group_tok[3])

    if KSTOP <= 2:
        return finalize()
    if DBG:
        dma("gpsimd", dbg_xn[:, :], x16[0][:, :], "gdbg", waits=[t_xn0])
    fc_upre(0, t_xn0)
    for c4 in range(4):
        while fc_group_tok[c4] is None:
            drain_q(1)
        fc_group(0, t_xn0, c4, fc_group_tok[c4])
        drain_q(16)
    drain_q(len(qplan))
    if DBG:
        dma("gpsimd", dbg_g1[:, :], g1[:, 0:8 * T], "gdbg",
            waits=[gelu_done[(0, 7)], upr_free[0]])
        dma("gpsimd", dbg_u[:, 0:T], ufc16[:, :], "gdbg", waits=[t_ufc16[0]])
        dma("gpsimd", dbg_u[:, T:2 * T], upr16[:, :], "gdbg", waits=[t_upr16[0]])
        t_dbg = ("sd", P.semcount["sd"])
        t_dbv = ("vd", P.semcount["vd"])
        dma("gpsimd", dbg_scl[:, :], scl_sb[:, :], "gdbg", waits=[t_dbg, t_dbv])
        dma("gpsimd", dbg_w0[:, :], weff_fc[0][:, :], "gdbg", waits=[t_dbg])
        dma("gpsimd", dbg_w5[:, :], weff_fc[5][:, :], "gdbg", waits=[t_dbg])
        dma("gpsimd", dbg_wp[:, :], wpr_eff[0:256, :], "gdbg", waits=[("sw0", P.semcount["sw0"]), ("sw1", P.semcount["sw1"])])
        dma("gpsimd", dbg_ats[:, 0:512], aprTs[:, :], "gdbg", waits=[t_dbv])
        dma("gpsimd", dbg_ats[:, 512:640], afcTs[:, :], "gdbg", waits=[t_dbv])

    if KSTOP <= 3:
        return finalize()

    if KSTOP <= 4:
        return finalize()
    t_fl = dma("gpsimd", flat_sb[0:1, :], ag_out.ap().rearrange("j c -> (j c)"),
               "gsc", waits=[t_ag])
    t_hi = P.op("vector", lambda e: e.memset(hin_row[:, :], 0.0), inc="vd")
    for j in range(8):
        As = flat_sb[0:1, 64 * j:64 * j + 32]
        Us = flat_sb[0:1, 64 * j + 32:64 * (j + 1)]
        sj = sel_sb[0:1, j:j + 1]
        t_t = P.op("vector", lambda e, As=As, sj=sj: e.tensor_scalar(
            tmp_row[0:1, 0:32], As, -1.0, sj, OP.add, OP.mult),
            waits=[t_fl, t_small, t_hi], inc="vd")
        t_u = P.op("vector", lambda e, Us=Us, sj=sj: e.tensor_single_scalar(
            tmp_row[0:1, 32:64], Us, sj, OP.mult),
            waits=[t_fl], inc="vd")
        t_m = P.op("vector", lambda e: e.tensor_mul(tmp_row[0:1, 0:32], tmp_row[0:1, 0:32],
                                                    hin_row[0:1, :]),
                   waits=[t_t], inc="vd")
        t_a1 = P.op("vector", lambda e: e.tensor_add(hin_row[0:1, :], hin_row[0:1, :],
                                                     tmp_row[0:1, 0:32]),
                    waits=[t_m], inc="vd")
        t_hi = P.op("vector", lambda e: e.tensor_add(hin_row[0:1, :], hin_row[0:1, :],
                                                     tmp_row[0:1, 32:64]),
                    waits=[t_a1, t_u], inc="vd")
    t_hw = dma("gpsimd", hin_d[0:1, :], hin_row[0:1, :], "gsc", waits=[t_hi])
    t_hr = dma("gpsimd", hin_col.ap(), hin_d[0:1, :].rearrange("o f -> f o"), "gsc",
               waits=[t_hw], slow=True)
    t_fix = P.op("vector", lambda e: e.scalar_tensor_tensor(
        scanB[:, :], scanA[:, :], hin_col[:, :], scanB[:, :], OP.mult, OP.add),
        waits=[t_hr], inc="vd")
    t_h16 = P.op("vector", lambda e: e.tensor_copy(h16[0:32, :], scanB[:, :]),
                 waits=[t_fix], inc="vd")
    if DBG:
        dma("gpsimd", dbg_h16[:, :], h16[:, :], "gdbg", waits=[t_h16, t_h16one])
        dma("gpsimd", dbg_sA[:, :], scanA[:, :], "gdbg", waits=[t_fix])

    wprs_tok = {}
    outst_free = [None, None]
    xfin_free = [a_toks[-1][0], None]
    t_store_last = None

    def projwoutfin(t):
        nonlocal t_store_last
        tb = t % 2
        for k in range(PKC):
            wb = 2 + (k % 4)
            if t == 0 and k < 4:
                prev_use = None
            elif t > 0 and k < 4:
                prev_use = wprs_tok.get((t - 1, PKC - 4 + k))
            else:
                prev_use = wprs_tok.get((t, k - 4))
            t_ldp = dma("sync", wprs[wb][:, :],
                        wpr_eff[128 * k:128 * (k + 1), :], f"pl{k % 4}",
                        waits=[spill_tok[k], prev_use])
            for m in range(PMC):
                w = [t_ldp, gelu_done[(t, k)]]
                if k == 0:
                    w.append(bankF[m] or bankD[m])
                t_mm = P.op("tensor",
                            lambda e, k=k, m=m, wb=wb: e.matmul(
                                pb[m][:, :], wprs[wb][:, 128 * m:128 * (m + 1)],
                                g1[:, T * k:T * (k + 1)],
                                start=(k == 0), stop=False),
                            waits=w, inc="td")
                if m == PMC - 1:
                    wprs_tok[(t, k)] = t_mm
        for m in range(PMC):
            t_lo = P.op("tensor",
                        lambda e, m=m: e.matmul(
                            pb[m][:, :], bprT16[:, 128 * m:128 * (m + 1)],
                            upr16[:, :], start=False, stop=True),
                        waits=[t_upr16[t], t_gsm], inc="td")
            bankD[m] = t_lo
        fin_toks = finals_of[t] = []
        th_free = None
        for m in range(PMC):
            gb = m % 2
            t_th = P.op("scalar",
                        lambda e, m=m: e.activation(out=th_s[:, :], in_=pb[m][:, :],
                                                    func=AF.Tanh,
                                                    scale=scl_sb[:, 4:5],
                                                    bias=bprh_sb[:, m:m + 1]),
                        waits=[bankD[m], t_bprh, t_sclh, th_free], inc="sd")
            t_p1 = P.op("scalar",
                        lambda e, m=m: e.activation(out=p1_s[:, :], in_=pb[m][:, :],
                                                    func=AF.Identity,
                                                    scale=scl_sb[:, 3:4],
                                                    bias=bpr_sb[:, m:m + 1]),
                        waits=[t_th], inc="sd")
            t_g2 = P.op("vector",
                        lambda e, gb=gb: e.scalar_tensor_tensor(
                            g_s[gb][:, :], th_s[:, :], 1.0, p1_s[:, :], OP.add, OP.mult),
                        waits=[t_p1], inc="vd")
            th_free = t_g2
            t_wo = P.op("tensor",
                        lambda e, m=m, t=t: e.matmul(pb[m][:, :],
                                                     woutb16[:, 128 * m:128 * (m + 1)],
                                                     h16[:, T * t:T * (t + 1)],
                                                     start=True, stop=True),
                        waits=[t_p1, t_th, t_h16, t_h16one, t_gsm], inc="td")
            bankD[m] = t_wo
            xb = m % 2
            t_xf = dma("scalar", xfin[xb][:, :],
                       xT[:, T * t:T * (t + 1)].rearrange("(k p) f -> k p f", p=128)[m],
                       f"xf{xb}", waits=[xfin_free[xb]])
            ob = m % 2
            t_f3 = P.op("vector",
                        lambda e, ob=ob, m=m, tb=tb: e.tensor_add(
                            outst[ob][:, :], x16[tb][:, T * m:T * (m + 1)], pb[m][:, :]),
                        waits=[t_wo, t_g2, outst_free[ob]], inc="vd")
            bankF[m] = t_f3
            t_f4 = P.op("vector",
                        lambda e, ob=ob, gb=gb: e.tensor_mul(outst[ob][:, :], outst[ob][:, :],
                                                             g_s[gb][:, :]),
                        inc="vd")
            t_f5 = P.op("vector",
                        lambda e, ob=ob, xb=xb: e.scalar_tensor_tensor(
                            outst[ob][:, :], outst[ob][:, :], 0.5, xfin[xb][:, :],
                            OP.mult, OP.add),
                        waits=[t_xf], inc="vd")
            xfin_free[xb] = t_f5
            fin_toks.append(t_f5)
            t_st = dma("gpsimd",
                       outT[:, T * t:T * (t + 1)].rearrange("(k p) f -> k p f", p=128)[m],
                       outst[ob][:, :], f"st{ob}", waits=[t_f5])
            outst_free[ob] = t_st
            t_store_last = t_st

    if KSTOP <= 5:
        return finalize()
    for t in range(NT):
        if t + 1 < NT:
            t_xn_next = prep(t + 1)
        projwoutfin(t)
        if t + 1 < NT:
            fc(t + 1, t_xn_next)
        if KSTOP <= 6 + t:
            return finalize()
    if KSTOP <= 9:
        return finalize()

    P.op("gpsimd", lambda e: None,
         waits=[("st0", P.semcount["st0"]), ("st1", P.semcount["st1"])], inc="done")

    with nc.Block() as block:
        P.emit_all(block)
    P._es.close()
    es.close()
    return nc


def make_inputs(full):
    x = np.asarray(full["L0_attn_x"], np.float32)
    W_a = np.asarray(full["L0_attn_W_a"], np.float32)
    b_a = np.asarray(full["L0_attn_b_a"], np.float32)
    W_in = np.asarray(full["L0_attn_W_in"], np.float32)
    b_in = np.asarray(full["L0_attn_b_in"], np.float32)
    W_out = np.asarray(full["L0_attn_W_out"], np.float32)
    b_out = np.asarray(full["L0_attn_b_out"], np.float32)
    gamma = np.asarray(full["L0_attn_ln_gamma"], np.float32)
    beta = np.asarray(full["L0_attn_ln_beta"], np.float32)
    W_fc = np.asarray(full["L0_ffn_W_fc"], np.float32)
    b_fc = np.asarray(full["L0_ffn_b_fc"], np.float32)
    A_fc = np.asarray(full["L0_ffn_A_fc"], np.float32)
    B_fc = np.asarray(full["L0_ffn_B_fc"], np.float32)
    W_pr = np.asarray(full["L0_ffn_W_proj"], np.float32)
    b_pr = np.asarray(full["L0_ffn_b_proj"], np.float32)
    A_pr = np.asarray(full["L0_ffn_A_proj"], np.float32)
    B_pr = np.asarray(full["L0_ffn_B_proj"], np.float32)

    C = np.ascontiguousarray
    shared = dict(
        wfcT=C(W_fc.T), wprT=C(W_pr.T), waT=C(W_a.T), wiT=C(W_in.T),
        woutbT=C(np.vstack([W_out.T, b_out[None, :]])),
        afcTd=C(A_fc.T.reshape(KC, 128, 16).transpose(1, 0, 2).reshape(128, 128)),
        bfcTd=C(B_fc.T),
        aprTd=C(A_pr.T.reshape(PKC, 128, 16).transpose(1, 0, 2).reshape(128, 512)),
        bprTd=C(B_pr.T),
        gamma8=C(gamma.reshape(KC, 128).T), beta8=C(beta.reshape(KC, 128).T),
        bfc32d=C(b_fc.reshape(MC, 128).T), bpr8d=C(b_pr.reshape(PMC, 128).T),
        babi2=C(np.stack([b_a, b_in], axis=1)),
    )
    wfcT_l = shared["wfcT"]
    wprT_l = shared["wprT"]
    maps = []
    for c in range(8):
        b, q = c // 4, c % 4
        sel = np.zeros((1, 8), np.float32)
        for j in range(8):
            if j // 4 == b and j % 4 < q:
                sel[0, j] = 1.0
        m = dict(shared)
        m.update(
            xT=C(x[b, TOK * q:TOK * (q + 1), :].T),
            sel8=sel,
            shard_fc=C(wfcT_l[128 * c:128 * (c + 1), :]),
            shard_pr=C(wprT_l[512 * c:512 * (c + 1), :]),
        )
        maps.append(m)
    return maps


def assemble(results):
    out = np.empty((2, 8192, D), np.float32)
    for c, r in enumerate(results):
        b, q = c // 4, c % 4
        out[b, TOK * q:TOK * (q + 1), :] = np.asarray(r["outT"]).T
    return out


from concourse.bass_utils import run_bass_kernel_spmd


def kernel(**inputs):
    nc = build_nc()
    maps = make_inputs(inputs)
    res = run_bass_kernel_spmd(nc, maps, core_ids=list(range(8)))
    return assemble(res.results)


# version 12
# speedup vs baseline: 1.0538x; 1.0538x over previous
import contextlib
import os
from collections import defaultdict

import numpy as np
import concourse.bass as bass
import concourse.mybir as mybir

F32 = mybir.dt.float32
BF16 = mybir.dt.bfloat16
I32 = mybir.dt.int32
AF = mybir.ActivationFunctionType
OP = mybir.AluOpType
AX = mybir.AxisListType

D, FF, TOK, NT, T = 1024, 4096, 2048, 4, 512
KC, MC, PMC, PKC = 8, 32, 8, 32
EPS = 1e-5
MAGIC = 1597463007.0


class Plan:
    def __init__(self, nc):
        self.nc = nc
        self.names = ["sync", "gpsimd", "vector", "scalar", "tensor"]
        self.ops = {e: [] for e in self.names}
        self.semcount = defaultdict(int)
        self.sems = {}
        self._es = contextlib.ExitStack()

    def sem(self, name):
        if name not in self.sems:
            self.sems[name] = self._es.enter_context(self.nc.semaphore(name))
        return self.sems[name]

    def op(self, engine, emit, waits=(), inc=None, inc_k=1):
        waits = tuple(w for w in waits if w is not None)
        tok = None
        if inc is not None:
            self.semcount[inc] += inc_k
            tok = (inc, self.semcount[inc])
        self.ops[engine].append((emit, waits, inc, inc_k))
        return tok

    def dma(self, engine, dst, src, sem, waits=(), slow=False):
        return self.op(engine,
                       lambda e, d=dst, s=src, sl=slow: e.dma_start(
                           out=d, in_=s, allow_slow_non_contiguous=sl),
                       waits=waits, inc=sem, inc_k=16)

    def emit_all(self, block):
        waited = {e: defaultdict(int) for e in self.names}

        def mk(engine_name):
            def _body(eng):
                for emit, waits, inc, inc_k in self.ops[engine_name]:
                    for (s, v) in waits:
                        if waited[engine_name][s] >= v:
                            continue
                        eng.wait_ge(self.sem(s), v)
                        waited[engine_name][s] = v
                    instr = emit(eng)
                    if inc is not None:
                        if instr is None:
                            instr = eng.nop()
                        instr.then_inc(self.sem(inc), inc_k)
            return _body

        block.sync(mk("sync"))
        block.gpsimd(mk("gpsimd"))
        block.vector(mk("vector"))
        block.scalar(mk("scalar"))
        block.tensor(mk("tensor"))


def build_nc():
    nc = bass.Bass("TRN2", target_bir_lowering=False, detect_race_conditions=False)

    xT = nc.dram_tensor("xT", [D, TOK], F32, kind="ExternalInput")
    wfcT = nc.dram_tensor("wfcT", [D, FF], F32, kind="ExternalInput")
    wprT = nc.dram_tensor("wprT", [FF, D], F32, kind="ExternalInput")
    waT = nc.dram_tensor("waT", [D, 32], F32, kind="ExternalInput")
    wiT = nc.dram_tensor("wiT", [D, 32], F32, kind="ExternalInput")
    woutbT = nc.dram_tensor("woutbT", [33, D], F32, kind="ExternalInput")
    afcTd = nc.dram_tensor("afcTd", [128, 128], F32, kind="ExternalInput")
    bfcTd = nc.dram_tensor("bfcTd", [16, FF], F32, kind="ExternalInput")
    aprTd = nc.dram_tensor("aprTd", [128, 512], F32, kind="ExternalInput")
    bprTd = nc.dram_tensor("bprTd", [16, D], F32, kind="ExternalInput")
    gamma8 = nc.dram_tensor("gamma8", [128, KC], F32, kind="ExternalInput")
    beta8 = nc.dram_tensor("beta8", [128, KC], F32, kind="ExternalInput")
    bfc32d = nc.dram_tensor("bfc32d", [128, MC], F32, kind="ExternalInput")
    bpr8d = nc.dram_tensor("bpr8d", [128, PMC], F32, kind="ExternalInput")
    babi2 = nc.dram_tensor("babi2", [32, 2], F32, kind="ExternalInput")
    sel8 = nc.dram_tensor("sel8", [1, 8], F32, kind="ExternalInput")
    shard_fc = nc.dram_tensor("shard_fc", [128, FF], F32, kind="ExternalInput")
    shard_pr = nc.dram_tensor("shard_pr", [512, D], F32, kind="ExternalInput")
    outT = nc.dram_tensor("outT", [D, TOK], F32, kind="ExternalOutput")
    DBG = os.environ.get("DBG") == "1"
    if DBG:
        dbg_scl = nc.dram_tensor("dbg_scl", [128, 5], F32, kind="ExternalOutput")
        dbg_w0 = nc.dram_tensor("dbg_w0", [128, FF], BF16, kind="ExternalOutput")
        dbg_w5 = nc.dram_tensor("dbg_w5", [128, FF], BF16, kind="ExternalOutput")
        dbg_wp = nc.dram_tensor("dbg_wp", [256, D], BF16, kind="ExternalOutput")
        dbg_ats = nc.dram_tensor("dbg_ats", [128, 512 + 128], BF16, kind="ExternalOutput")
        dbg_xn = nc.dram_tensor("dbg_xn", [128, KC * T], BF16, kind="ExternalOutput")
        dbg_g1 = nc.dram_tensor("dbg_g1", [128, 8 * T], BF16, kind="ExternalOutput")
        dbg_u = nc.dram_tensor("dbg_u", [16, 2 * T], BF16, kind="ExternalOutput")
        dbg_h16 = nc.dram_tensor("dbg_h16", [33, TOK], BF16, kind="ExternalOutput")
        dbg_sA = nc.dram_tensor("dbg_sA", [32, TOK], F32, kind="ExternalOutput")
        dbg_rawB = nc.dram_tensor("dbg_rawB", [32, TOK], F32, kind="ExternalOutput")
        dbg_rawA = nc.dram_tensor("dbg_rawA", [32, TOK], F32, kind="ExternalOutput")
    wpr_eff = nc.dram_tensor("wpr_eff", [FF, D], BF16)
    am_in = nc.dram_tensor("am_in", [2, 1], F32)
    am_out = nc.dram_tensor("am_out", [2, 1], F32)
    ag_in = nc.dram_tensor("ag_in", [2, 32], F32)
    ag_out = nc.dram_tensor("ag_out", [16, 32], F32)
    amT_d = nc.dram_tensor("amT_d", [2, 128], F32)
    hin_d = nc.dram_tensor("hin_d", [1, 32], F32)
    rows_d = [nc.dram_tensor(f"rows_d{t}", [1, 1536], F32) for t in range(NT)]

    es = contextlib.ExitStack()
    E = es.enter_context

    weff_fc = [E(nc.sbuf_tensor(f"weff_fc{k}", [128, FF], BF16)) for k in range(KC)]
    g1 = E(nc.sbuf_tensor("g1", [128, PKC * T], BF16))
    x16 = [E(nc.sbuf_tensor(f"x16_{i}", [128, KC * T], BF16)) for i in range(2)]
    wstr = [E(nc.sbuf_tensor(f"wstr{i}", [128, 512], F32)) for i in range(6)]
    wprs = [E(nc.sbuf_tensor(f"wprs{i}", [128, D], BF16)) for i in range(6)]
    ki = [E(nc.sbuf_tensor(f"ki{i}", [128, 512], I32)) for i in range(2)]
    scanA = E(nc.sbuf_tensor("scanA", [32, TOK], F32))
    scanB = E(nc.sbuf_tensor("scanB", [32, TOK], F32))
    h16 = E(nc.sbuf_tensor("h16", [33, TOK], BF16))
    base_g1 = nc.lookup_mloc(g1).addr
    rows = nc.alloc_sbuf_tensor_at("rows", [1, 2048], F32, offset=base_g1)
    rows16 = nc.alloc_sbuf_tensor_at("rows16", [1, 512], BF16, offset=base_g1 + 8192)
    t1row = nc.alloc_sbuf_tensor_at("t1row", [1, 512], F32, offset=base_g1 + 9216)
    yirow = nc.alloc_sbuf_tensor_at("yirow", [1, 512], I32, offset=base_g1 + 11264)
    bc16 = E(nc.sbuf_tensor("bc16", [128, T], BF16))
    r1b16 = E(nc.sbuf_tensor("r1b16", [128, T], BF16))
    m1r1b16 = E(nc.sbuf_tensor("m1r1b16", [128, T], BF16))
    xr16 = [E(nc.sbuf_tensor(f"xr16_{i}", [128, T], BF16)) for i in range(2)]
    sq16 = [E(nc.sbuf_tensor(f"sq16_{i}", [128, T], BF16)) for i in range(2)]
    xfin = [E(nc.sbuf_tensor(f"xfin{i}", [128, T], F32)) for i in range(2)]
    bfcT16 = E(nc.sbuf_tensor("bfcT16", [16, FF], BF16))
    apr16 = E(nc.sbuf_tensor("apr16", [16, FF], BF16))
    g_s = [E(nc.sbuf_tensor(f"g_s{i}", [128, T], F32)) for i in range(2)]
    wab16 = E(nc.sbuf_tensor("wab16", [128, KC * 64], BF16))
    woutb16 = E(nc.sbuf_tensor("woutb16", [33, D], BF16))
    afcTs = E(nc.sbuf_tensor("afcTs", [128, 128], BF16))
    aprTs = E(nc.sbuf_tensor("aprTs", [128, 512], BF16))
    afcTf = E(nc.sbuf_tensor("afcTf", [128, 128], F32))
    aprTf = E(nc.sbuf_tensor("aprTf", [128, 512], F32))
    ufc16 = E(nc.sbuf_tensor("ufc16", [16, T], BF16))
    upr16 = E(nc.sbuf_tensor("upr16", [16, T], BF16))
    bprT16 = E(nc.sbuf_tensor("bprT16", [16, D], BF16))
    gam_sb = E(nc.sbuf_tensor("gam_sb", [128, KC], F32))
    bet_sb = E(nc.sbuf_tensor("bet_sb", [128, KC], F32))
    bet16 = E(nc.sbuf_tensor("bet16", [128, KC], BF16))
    bfc_sb = E(nc.sbuf_tensor("bfc_sb", [128, MC], F32))
    bpr_sb = E(nc.sbuf_tensor("bpr_sb", [128, PMC], F32))
    bprh_sb = E(nc.sbuf_tensor("bprh_sb", [128, PMC], F32))
    babi_sb = E(nc.sbuf_tensor("babi_sb", [32, 2], F32))
    dcol2 = E(nc.sbuf_tensor("dcol2", [32, 2], F32))
    halfda = E(nc.sbuf_tensor("halfda", [32, 1], F32))
    cg16 = E(nc.sbuf_tensor("cg16", [1, 64], BF16))
    sel_sb = E(nc.sbuf_tensor("sel_sb", [1, 8], F32))
    ones16 = E(nc.sbuf_tensor("ones16", [128, 1], BF16))
    amax_sb = E(nc.sbuf_tensor("amax_sb", [128, 8], F32))
    amT_sb = E(nc.sbuf_tensor("amT_sb", [2, 128], F32))
    am2_sb = E(nc.sbuf_tensor("am2_sb", [2, 1], F32))
    amax_bc = E(nc.sbuf_tensor("amax_bc", [128, 2], F32))
    inv_bc = E(nc.sbuf_tensor("inv_bc", [128, 2], F32))
    scl_sb = E(nc.sbuf_tensor("scl_sb", [128, 5], F32))
    amscr = E(nc.sbuf_tensor("amscr", [128, 1], F32))
    flat_sb = E(nc.sbuf_tensor("flat_sb", [1, 512], F32))
    hin_row = E(nc.sbuf_tensor("hin_row", [1, 32], F32))
    hin_col = E(nc.sbuf_tensor("hin_col", [32, 1], F32))
    tmp_row = E(nc.sbuf_tensor("tmp_row", [1, 64], F32))

    base_a = nc.lookup_mloc(apr16).addr
    th_s = nc.alloc_sbuf_tensor_at("th_s", [128, T], F32, offset=base_a)
    p1_s = nc.alloc_sbuf_tensor_at("p1_s", [128, T], F32, offset=base_a + 2048)
    outst = [nc.alloc_sbuf_tensor_at(f"outst{i}", [128, T], F32, offset=base_a + 4096 + 2048 * i)
             for i in range(2)]

    pb = [E(nc.psum_tensor(f"pb{i}", [128, T], F32)) for i in range(8)]

    P = Plan(nc)
    dma = P.dma

    t_small = None
    for dst, src in [
        (gam_sb[:, :], gamma8[:, :]), (bet_sb[:, :], beta8[:, :]),
        (bfc_sb[:, :], bfc32d[:, :]), (bpr_sb[:, :], bpr8d[:, :]),
        (babi_sb[:, :], babi2[:, :]), (sel_sb[:, :], sel8[:, :]),
        (afcTf[:, :], afcTd[:, :]), (aprTf[:, :], aprTd[:, :]),
    ]:
        t_small = dma("sync", dst, src, "lsm")
    dma("gpsimd", wab16.ap().rearrange("p (k c) -> p k c", c=64)[:, :, 0:32],
        waT.ap().rearrange("(k p) c -> p k c", p=128), "gsm")
    dma("gpsimd", wab16.ap().rearrange("p (k c) -> p k c", c=64)[:, :, 32:64],
        wiT.ap().rearrange("(k p) c -> p k c", p=128), "gsm")
    dma("gpsimd", woutb16[:, :], woutbT[:, :], "gsm")
    dma("gpsimd", bfcT16[:, :], bfcTd[:, :], "gsm")
    t_gsm = dma("gpsimd", bprT16[:, :], bprTd[:, :], "gsm")
    t_ones = P.op("vector", lambda e: e.memset(ones16[:, :], 1.0), inc="vd")
    t_h16one = P.op("vector", lambda e: e.memset(h16[32:33, :], 1.0), inc="vd")
    t_bet16 = P.op("vector", lambda e: e.tensor_copy(bet16[:, :], bet_sb[:, :]),
                   waits=[t_small], inc="vd")
    t_bprh = P.op("vector", lambda e: e.tensor_scalar_mul(bprh_sb[:, :], bpr_sb[:, :], 0.5),
                  waits=[t_small], inc="vd")

    t_piece = []
    piece_srcs = ([shard_fc[:, 1024 * i:1024 * (i + 1)] for i in range(4)] +
                  [shard_pr[128 * j:128 * (j + 1), :] for j in range(4)])
    prev_red = [None, None]
    for idx, psrc in enumerate(piece_srcs):
        t_l0 = dma("sync", wstr[0][:, :], psrc[:, 0:512], "ws0", waits=[prev_red[0]])
        t_l1 = dma("sync", wstr[1][:, :], psrc[:, 512:1024], "ws1", waits=[prev_red[1]])
        t_r0 = P.op("vector",
                    lambda e, idx=idx: e.tensor_reduce(
                        amax_sb[:, idx:idx + 1], wstr[0][:, :], AX.X, OP.max,
                        apply_absolute_value=True),
                    waits=[t_l0], inc="vd")
        t_r1 = P.op("vector",
                    lambda e: e.tensor_reduce(
                        amscr[:, :], wstr[1][:, :], AX.X, OP.max,
                        apply_absolute_value=True),
                    waits=[t_l1], inc="vd")
        t_rc = P.op("vector",
                    lambda e, idx=idx: e.tensor_max(
                        amax_sb[:, idx:idx + 1], amax_sb[:, idx:idx + 1], amscr[:, :]),
                    waits=[t_r0, t_r1], inc="vd")
        prev_red[0] = t_rc
        prev_red[1] = t_rc
        t_piece.append(t_rc)
    t_c1 = P.op("vector", lambda e: e.tensor_max(amax_sb[:, 0:1], amax_sb[:, 0:1], amax_sb[:, 1:2]),
                waits=[t_piece[1]], inc="vd")
    t_c2 = P.op("vector", lambda e: e.tensor_max(amax_sb[:, 2:3], amax_sb[:, 2:3], amax_sb[:, 3:4]),
                waits=[t_piece[3]], inc="vd")
    t_c3 = P.op("vector", lambda e: e.tensor_max(amax_sb[:, 0:1], amax_sb[:, 0:1], amax_sb[:, 2:3]),
                waits=[t_c1, t_c2], inc="vd")
    t_c4 = P.op("vector", lambda e: e.tensor_max(amax_sb[:, 4:5], amax_sb[:, 4:5], amax_sb[:, 5:6]),
                waits=[t_piece[5]], inc="vd")
    t_c5 = P.op("vector", lambda e: e.tensor_max(amax_sb[:, 6:7], amax_sb[:, 6:7], amax_sb[:, 7:8]),
                waits=[t_piece[7]], inc="vd")
    t_c6 = P.op("vector", lambda e: e.tensor_max(amax_sb[:, 1:2], amax_sb[:, 4:5], amax_sb[:, 6:7]),
                waits=[t_c4, t_c5, t_c3], inc="vd")
    t_amw = dma("gpsimd", amT_d.ap().rearrange("c p -> p c"), amax_sb[:, 0:2], "gam",
                waits=[t_c6], slow=True)
    t_amr = dma("gpsimd", amT_sb[:, :], amT_d[:, :], "gam", waits=[t_amw])
    t_am2 = P.op("vector", lambda e: e.tensor_reduce(am2_sb[:, :], amT_sb[:, :], AX.X, OP.max),
                 waits=[t_amr], inc="vd")
    t_amb = dma("gpsimd", am_in[:, :], am2_sb[:, :], "gam", waits=[t_am2])
    t_ar = P.op("gpsimd", lambda e: e.collective_compute(
        "AllReduce", OP.max, replica_groups=[list(range(8))],
        ins=[am_in.ap().opt()], outs=[am_out.ap().opt()]),
        waits=[t_amb], inc="cc1")
    t_amg = dma("gpsimd", am2_sb[:, :], am_out[:, :], "gam", waits=[t_ar])
    t_abc = dma("gpsimd", amax_bc.ap(), am_out.ap().rearrange("a b -> b a").broadcast_to([128, 2]),
                "gam", waits=[t_ar, t_amg])

    t_da_mm = None
    for k in range(KC):
        t_da_mm = P.op("tensor",
                       lambda e, k=k: e.matmul(pb[5][0:32, 0:1],
                                               wab16[:, 64 * k:64 * k + 32], bet16[:, k:k + 1],
                                               start=(k == 0), stop=(k == KC - 1)),
                       waits=[t_gsm, t_bet16], inc="td")
    t_da = P.op("vector", lambda e: e.tensor_add(dcol2[:, 0:1], pb[5][0:32, 0:1], babi_sb[:, 0:1]),
                waits=[t_da_mm, t_small], inc="vd")
    t_db_mm = None
    for k in range(KC):
        t_db_mm = P.op("tensor",
                       lambda e, k=k: e.matmul(pb[5][0:32, 1:2],
                                               wab16[:, 64 * k + 32:64 * k + 64], bet16[:, k:k + 1],
                                               start=(k == 0), stop=(k == KC - 1)),
                       waits=[t_da], inc="td")
    t_db = P.op("vector", lambda e: e.tensor_add(dcol2[:, 1:2], pb[5][0:32, 1:2], babi_sb[:, 1:2]),
                waits=[t_db_mm], inc="vd")
    t_hda = P.op("vector", lambda e: e.tensor_scalar_mul(halfda[:, :], dcol2[:, 0:1], 0.5),
                 waits=[t_da], inc="vd")
    t_wg = None
    for k in range(KC):
        t_wg = P.op("vector",
                    lambda e, k=k: e.tensor_scalar_mul(wab16[:, 64 * k:64 * (k + 1)],
                                                       wab16[:, 64 * k:64 * (k + 1)],
                                                       gam_sb[:, k:k + 1]),
                    waits=[t_db_mm, t_small], inc="vd")
    t_cg_mm = None
    for k in range(KC):
        t_cg_mm = P.op("tensor",
                       lambda e, k=k: e.matmul(pb[5][0:1, 0:64], ones16[:, :],
                                               wab16[:, 64 * k:64 * (k + 1)],
                                               start=(k == 0), stop=(k == KC - 1)),
                       waits=[t_wg, t_ones, t_db], inc="td")
    t_cg = P.op("vector", lambda e: e.tensor_copy(cg16[:, :], pb[5][0:1, 0:64]),
                waits=[t_cg_mm], inc="vd")

    t_scl = t_sclh = t_afcs = t_aprs = None

    def emit_scl_ops():
        nonlocal t_scl, t_sclh, t_afcs, t_aprs
        t_rec = P.op("vector", lambda e: e.reciprocal(inv_bc[:, :], amax_bc[:, :]),
                     waits=[t_abc], inc="vd")
        t_q1 = P.op("vector", lambda e: e.tensor_scalar_mul(scl_sb[:, 0:1], inv_bc[:, 0:1], 127.0),
                    waits=[t_rec], inc="vd")
        t_q2 = P.op("vector", lambda e: e.tensor_scalar_mul(scl_sb[:, 1:2], amax_bc[:, 0:1], 1.0 / 127.0),
                    waits=[t_abc], inc="vd")
        t_q3 = P.op("vector", lambda e: e.tensor_scalar_mul(scl_sb[:, 2:3], inv_bc[:, 1:2], 127.0),
                    waits=[t_rec], inc="vd")
        t_scl = P.op("vector", lambda e: e.tensor_scalar_mul(scl_sb[:, 3:4], amax_bc[:, 1:2], 1.0 / 127.0),
                     waits=[t_q1, t_q2, t_q3], inc="vd")
        t_sclh = P.op("vector", lambda e: e.tensor_scalar_mul(scl_sb[:, 4:5], scl_sb[:, 3:4], 0.5),
                      waits=[t_scl], inc="vd")
        t_afcs = P.op("vector", lambda e: e.tensor_scalar(
            afcTs[:, :], afcTf[:, :], scl_sb[:, 0:1], 2.0, OP.mult, OP.mult),
            waits=[t_scl, t_small], inc="vd")
        t_aprs = P.op("vector", lambda e: e.tensor_scalar(
            aprTs[:, :], aprTf[:, :], scl_sb[:, 2:3], 2.0, OP.mult, OP.mult),
            waits=[t_afcs], inc="vd")

    wstr_free = [prev_red[0], prev_red[1], None, None, None, None]
    ki_free = [None, None]
    fc_group_tok = [None] * 4
    spill_tok = [None] * PKC
    wprs_free = [None, None]

    def quant_quarter(j, src_ap, dst_ap, invs_ap, extra_dst_waits=()):
        buf = j % 6
        kb = j % 2
        t_l = dma("sync", wstr[buf][:, :], src_ap, f"ws{buf}", waits=[wstr_free[buf]])
        t_k = P.op("vector",
                   lambda e, b=buf, kb=kb, ia=invs_ap: e.tensor_scalar_mul(
                       ki[kb][:, :], wstr[b][:, :], ia),
                   waits=[t_l, t_scl, ki_free[kb]], inc="vd")
        t_w = P.op("scalar",
                   lambda e, da=dst_ap, kb=kb: e.activation(
                       out=da, in_=ki[kb][:, :], func=AF.Identity),
                   waits=[t_k] + list(extra_dst_waits), inc="sd")
        wstr_free[buf] = t_k
        ki_free[kb] = t_w
        return t_w

    qplan = []
    for c4 in range(4):
        for k in range(KC):
            for q in range(2):
                col = slice(1024 * c4 + 512 * q, 1024 * c4 + 512 * (q + 1))
                qplan.append((wfcT[128 * k:128 * (k + 1), col], weff_fc[k][:, col],
                              0, None))
    for ch in range(PKC):
        for q in range(2):
            qplan.append((wprT[128 * ch:128 * (ch + 1), 512 * q:512 * (q + 1)],
                          wprs[ch % 2][:, 512 * q:512 * (q + 1)], 2, ch))
    qidx = [0]

    def drain_q(n):
        while n > 0 and qidx[0] < len(qplan):
            j = qidx[0]
            qidx[0] += 1
            n -= 1
            src, dst, sc, ch = qplan[j]
            extra = ()
            if ch is not None and j % 2 == 0 and wprs_free[ch % 2] is not None:
                extra = (wprs_free[ch % 2],)
            t_w = quant_quarter(j, src, dst, scl_sb[:, sc:sc + 1],
                                extra_dst_waits=extra)
            if ch is None and j % 16 == 15:
                fc_group_tok[j // 16] = t_w
            if ch is not None and j % 2 == 1:
                wb = ch % 2
                t_wr = dma("scalar", wpr_eff[128 * ch:128 * (ch + 1), :],
                           wprs[wb][:, :], f"sw{wb}", waits=[t_w])
                spill_tok[ch] = t_wr
                wprs_free[wb] = t_wr

    bankD = [None] * 8
    bankF = [None] * 8
    bankF[5] = t_cg
    x16_free = [None, None]
    sq_free = [None, None]
    xr_free = [None, None]
    alpha_toks = []
    t_x16 = {}
    rw_prev = [None]

    def x16_load(t, tb, extra=()):
        return dma("gpsimd",
                   x16[tb].ap().rearrange("p (k f) -> p k f", f=T),
                   xT[:, T * t:T * (t + 1)].rearrange("(k p) f -> p k f", p=128),
                   f"xa{tb}", waits=extra)

    t_x16[0] = x16_load(0, 0)
    t_x16[1] = x16_load(1, 1)

    def newton_rsqrt(src_slice, out_slice, waits):
        t0 = P.op("vector", lambda e: e.tensor_single_scalar(
            yirow[:, :], rows[0:1, src_slice].bitcast(I32), 1, OP.arith_shift_right),
            waits=waits, inc="vd")
        last = P.op("vector", lambda e: e.tensor_scalar(
            yirow[:, :], yirow[:, :], -1.0, MAGIC, OP.mult, OP.add),
            waits=[t0], inc="vd")
        for it in range(3):
            a = P.op("vector", lambda e: e.tensor_mul(t1row[:, :], yirow.ap().bitcast(F32),
                                                      yirow.ap().bitcast(F32)),
                     waits=[last], inc="vd")
            b = P.op("vector", lambda e: e.tensor_mul(t1row[:, :], t1row[:, :],
                                                      rows[0:1, src_slice]),
                     waits=[a], inc="vd")
            c = P.op("vector", lambda e: e.tensor_scalar(t1row[:, :], t1row[:, :], -0.5, 1.5,
                                                         OP.mult, OP.add),
                     waits=[b], inc="vd")
            last = P.op("vector", lambda e: e.tensor_mul(yirow[:, :].bitcast(F32),
                                                         yirow.ap().bitcast(F32), t1row[:, :]),
                        waits=[c], inc="vd")
        return P.op("vector", lambda e: e.tensor_copy(rows[0:1, out_slice],
                                                      yirow.ap().bitcast(F32)),
                    waits=[last], inc="vd")

    r1s, m1r1s, r12s, scs = (slice(0, 512), slice(512, 1024),
                             slice(1024, 1536), slice(1536, 2048))

    stats_toks = {}

    def tileA_stats(t):
        tb = t % 2
        bs, bq = (3, 4) if t % 2 == 0 else (0, 1)
        tok_x = t_x16[t]
        t_sum = None
        for k in range(KC):
            w = [tok_x, t_ones, bankF[bs]] if k == 0 else []
            t_sum = P.op("tensor",
                         lambda e, k=k, tb=tb, bs=bs: e.matmul(pb[bs][0:1, :], ones16[:, :],
                                                               x16[tb][:, T * k:T * (k + 1)],
                                                               start=(k == 0), stop=(k == KC - 1)),
                         waits=w, inc="td")
        t_sqmm = None
        for k in range(KC):
            sb = k % 2
            t_sq = P.op("scalar",
                        lambda e, k=k, tb=tb, sb=sb: e.activation(
                            out=sq16[sb][:, :], in_=x16[tb][:, T * k:T * (k + 1)], func=AF.Square),
                        waits=[tok_x, sq_free[sb]], inc="sd")
            w = [t_sq] + ([bankF[bq]] if k == 0 else [])
            t_sqmm = P.op("tensor",
                          lambda e, k=k, sb=sb, bq=bq: e.matmul(pb[bq][0:1, :], ones16[:, :],
                                                                sq16[sb][:, :],
                                                                start=(k == 0), stop=(k == KC - 1)),
                          waits=w, inc="td")
            sq_free[sb] = t_sqmm
        stats_toks[t] = (t_sum, t_sqmm, bs, bq)

    def tileA(t):
        tb = t % 2
        t_sum, t_sqmm, bs, bq = stats_toks[t]
        bankD[bs], bankD[bq] = t_sum, t_sqmm
        R = rows
        t_m1 = P.op("vector", lambda e, bs=bs: e.tensor_scalar_mul(R[0:1, scs], pb[bs][0:1, :], 1.0 / D),
                    waits=[t_sum, rw_prev[0]], inc="vd")
        t_va = P.op("vector", lambda e, bq=bq: e.tensor_scalar_mul(R[0:1, r1s], pb[bq][0:1, :], 1.0 / D),
                    waits=[t_sqmm], inc="vd")
        t_vb = P.op("vector", lambda e: e.tensor_mul(R[0:1, m1r1s], R[0:1, scs], R[0:1, scs]),
                    waits=[t_m1], inc="vd")
        t_v1 = P.op("vector", lambda e: e.tensor_sub(R[0:1, r1s], R[0:1, r1s], R[0:1, m1r1s]),
                    waits=[t_va, t_vb], inc="vd")
        bankF[bs], bankF[bq] = t_m1, t_va
        t_a1 = P.op("vector", lambda e: e.tensor_scalar_add(R[0:1, r12s], R[0:1, r1s], EPS),
                    waits=[t_v1], inc="vd")
        t_r1 = newton_rsqrt(r12s, r12s, [t_a1])
        t_q = P.op("vector", lambda e: e.tensor_mul(R[0:1, m1r1s], R[0:1, r12s], R[0:1, r12s]),
                   waits=[t_r1], inc="vd")
        t_q2_ = P.op("vector", lambda e: e.tensor_mul(R[0:1, m1r1s], R[0:1, m1r1s], R[0:1, r1s]),
                     waits=[t_q], inc="vd")
        t_q3_ = P.op("vector", lambda e: e.tensor_scalar_add(R[0:1, m1r1s], R[0:1, m1r1s], EPS),
                     waits=[t_q2_], inc="vd")
        t_mvr1 = P.op("vector", lambda e: e.tensor_copy(R[0:1, r1s], R[0:1, r12s]),
                      waits=[t_q3_], inc="vd")
        t_r2 = newton_rsqrt(m1r1s, r12s, [t_mvr1])
        t_R12 = P.op("vector", lambda e: e.tensor_mul(R[0:1, r12s], R[0:1, r12s], R[0:1, r1s]),
                     waits=[t_r2], inc="vd")
        t_m1r1 = P.op("vector", lambda e: e.tensor_mul(R[0:1, m1r1s], R[0:1, scs], R[0:1, r1s]),
                      waits=[t_R12], inc="vd")
        t_S = P.op("vector", lambda e: e.scalar_tensor_tensor(
            R[0:1, scs], R[0:1, scs], -1.0, R[0:1, r12s], OP.mult, OP.mult),
            waits=[t_m1r1], inc="vd")
        t_S16 = P.op("vector", lambda e: e.tensor_copy(rows16[0:1, :], R[0:1, scs]),
                     waits=[t_S], inc="vd")
        t_rw = dma("gpsimd", rows_d[t][0:1, :], R[0:1, 0:1536], "grw", waits=[t_S])
        rw_prev[0] = t_rw
        t_bc3 = dma("gpsimd", bc16.ap(), rows_d[t][0:1, 1024:1536].broadcast_to([128, 512]),
                    "gbc", waits=[t_rw] + alpha_toks[-1:])
        t_za = t_zb = None
        for k in range(KC):
            xb = k % 2
            t_xr = P.op("vector",
                        lambda e, k=k, tb=tb, xb=xb: e.tensor_mul(
                            xr16[xb][:, :], x16[tb][:, T * k:T * (k + 1)], bc16[:, :]),
                        waits=[t_bc3, xr_free[xb]], inc="vd")
            w = [t_xr, t_wg] + ([bankF[2]] if k == 0 else [])
            t_za = P.op("tensor",
                        lambda e, k=k, xb=xb: e.matmul(pb[2][0:32, :],
                                                       wab16[:, 64 * k:64 * k + 32],
                                                       xr16[xb][:, :],
                                                       start=(k == 0), stop=False),
                        waits=w, inc="td")
            w2 = [bankF[5]] if k == 0 else []
            t_zb = P.op("tensor",
                        lambda e, k=k, xb=xb: e.matmul(pb[5][0:32, :],
                                                       wab16[:, 64 * k + 32:64 * (k + 1)],
                                                       xr16[xb][:, :],
                                                       start=(k == 0), stop=False),
                        waits=w2, inc="td")
            xr_free[xb] = t_zb
        t_za = P.op("tensor", lambda e: e.matmul(pb[2][0:32, :], cg16[0:1, 0:32],
                                                 rows16[0:1, :], start=False, stop=True),
                    waits=[t_S16, t_cg], inc="td")
        t_zb = P.op("tensor", lambda e: e.matmul(pb[5][0:32, :], cg16[0:1, 32:64],
                                                 rows16[0:1, :], start=False, stop=True),
                    inc="td")
        bankD[2], bankD[5] = t_za, t_zb
        t_th = P.op("scalar", lambda e: e.activation(out=xfin[0][0:32, :], in_=pb[2][0:32, :],
                                                     func=AF.Tanh, scale=0.5,
                                                     bias=halfda[:, :]),
                    waits=[t_za, t_hda], inc="sd")
        t_al = P.op("vector",
                    lambda e, t=t: e.tensor_scalar(scanA[:, T * t:T * (t + 1)], xfin[0][0:32, :],
                                                   0.5, 0.5, OP.mult, OP.add),
                    waits=[t_th], inc="vd")
        t_b = P.op("scalar",
                   lambda e, t=t: e.activation(out=scanB[:, T * t:T * (t + 1)],
                                               in_=pb[5][0:32, :], func=AF.Identity,
                                               bias=dcol2[:, 1:2]),
                   waits=[t_zb, t_db], inc="sd")
        bankF[2], bankF[5] = t_al, t_b
        alpha_toks.append(t_b)
        x16_free[tb] = t_zb
        return t_al, t_b

    a_toks = []
    tileA_stats(0)
    for t in range(NT):
        if t + 1 < NT:
            tileA_stats(t + 1)
        a_toks.append(tileA(t))
        if t + 2 < NT:
            t_x16[t + 2] = x16_load(t + 2, t % 2, extra=[x16_free[t % 2]])
        if t == 1:
            emit_scl_ops()
            drain_q(14)
        elif t == 2:
            drain_q(22)
        elif t == 3:
            drain_q(24)

    t_all_a = a_toks[-1][0]
    t_all_b = a_toks[-1][1]
    t_raw = None
    if DBG:
        dma("gpsimd", dbg_rawB[:, :], scanB[:, :], "gdbg", waits=[t_all_a, t_all_b])
        t_raw = dma("gpsimd", dbg_rawA[:, :], scanA[:, :], "gdbg", waits=[t_all_a, t_all_b])
    t_hl = P.op("vector", lambda e: e.tensor_tensor_scan(
        scanB[:, :], scanA[:, :], scanB[:, :], 0.0, OP.mult, OP.add),
        waits=[t_all_a, t_all_b, t_raw], inc="vd")
    t_cp = P.op("vector", lambda e: e.tensor_tensor_scan(
        scanA[:, :], scanA[:, :], scanA[:, :], 1.0, OP.mult, OP.bypass),
        waits=[t_hl], inc="vd")
    t_sw1 = dma("gpsimd", ag_in[0:1, :].rearrange("o f -> f o"), scanA[:, TOK - 1:TOK],
                "gsc", waits=[t_cp], slow=True)
    t_sw2 = dma("gpsimd", ag_in[1:2, :].rearrange("o f -> f o"), scanB[:, TOK - 1:TOK],
                "gsc", waits=[t_hl], slow=True)
    t_ag = P.op("gpsimd", lambda e: e.collective_compute(
        "AllGather", OP.bypass, replica_groups=[list(range(8))],
        ins=[ag_in.ap().opt()], outs=[ag_out.ap().opt()]),
        waits=[t_sw1, t_sw2], inc="cc2")

    finals_of = {}
    prep_done = {}

    def prep(t):
        tb = t % 2
        prev_fin_dep = finals_of.get(t - 2, [])
        t_bca = dma("gpsimd", r1b16.ap(), rows_d[t][0:1, 0:512].broadcast_to([128, 512]),
                    "gb32", waits=[prep_done.get(t - 1), rw_prev[0]])
        t_bcb = dma("gpsimd", m1r1b16.ap(), rows_d[t][0:1, 512:1024].broadcast_to([128, 512]),
                    "gb32")
        t_pair = t_bcb
        t_xl = x16_load(t, tb, extra=[x16_free[tb], a_toks[-1][1]] + prev_fin_dep[-1:])
        t_xn = None
        for k in range(KC):
            P.op("vector",
                 lambda e, k=k, tb=tb: e.tensor_mul(
                     x16[tb][:, T * k:T * (k + 1)], x16[tb][:, T * k:T * (k + 1)],
                     r1b16[:, :]),
                 waits=[t_xl, t_pair], inc="vd")
            t_xn = P.op("vector",
                        lambda e, k=k, tb=tb: e.tensor_sub(
                            x16[tb][:, T * k:T * (k + 1)], x16[tb][:, T * k:T * (k + 1)],
                            m1r1b16[:, :]),
                        inc="vd")
        prep_done[t] = t_xn
        return t_xn


    def finalize():
        P.op("gpsimd", lambda e: None,
             waits=[(s, P.semcount[s]) for s in ("vd", "sd", "td") if P.semcount[s]],
             inc="done")
        with nc.Block() as block:
            P.emit_all(block)
        P._es.close()
        es.close()
        return nc

    KSTOP = int(os.environ.get("KSTOP", "9"))
    if KSTOP <= 1:
        return finalize()

    t_xn0 = prep(0)

    gelu_done = {}
    t_ufc16 = {}
    t_upr16 = {}
    ufc_free = [None]
    upr_free = [None]

    def fc_upre(t, t_xn):
        tb = t % 2
        t_u = None
        for k in range(KC):
            w = [t_xn, t_afcs] + ([ufc_free[0], bankF[6]] if k == 0 else [])
            t_u = P.op("tensor",
                       lambda e, k=k, tb=tb: e.matmul(
                           pb[6][0:16, :], afcTs[:, 16 * k:16 * (k + 1)],
                           x16[tb][:, T * k:T * (k + 1)],
                           start=(k == 0), stop=(k == KC - 1)),
                       waits=w, inc="td")
        t_ufc16[t] = P.op("scalar",
                          lambda e: e.activation(out=ufc16[:, :], in_=pb[6][0:16, :],
                                                 func=AF.Identity),
                          waits=[t_u], inc="sd")
        ufc_free[0] = t_ufc16[t]

    fc_pend = {}

    def fc_group(t, t_xn, c4, gate_tok):
        tb = t % 2
        pend_up = fc_pend.setdefault(t, [])

        def emit_up(m, t_g):
            t_up = P.op("tensor",
                        lambda e, m=m: e.matmul(
                            pb[7][0:16, :], aprTs[:, 16 * m:16 * (m + 1)],
                            g1[:, T * m:T * (m + 1)],
                            start=(m == 0), stop=(m == MC - 1)),
                        waits=[t_g, t_aprs] + ([upr_free[0], bankF[7]] if m == 0 else []),
                        inc="td")
            if m == MC - 1:
                t_upr16[t] = P.op("scalar",
                                  lambda e: e.activation(out=upr16[:, :],
                                                         in_=pb[7][0:16, :],
                                                         func=AF.Identity),
                                  waits=[t_up], inc="sd")
                upr_free[0] = t_upr16[t]

        for m in range(8 * c4, 8 * c4 + 8):
            bk = m % 2
            w = [t_xn, bankF[bk] or bankD[bk], gate_tok]
            t_mm = None
            for k in range(KC):
                t_mm = P.op("tensor",
                            lambda e, k=k, m=m, tb=tb, bk=bk: e.matmul(
                                pb[bk][:, :], weff_fc[k][:, 128 * m:128 * (m + 1)],
                                x16[tb][:, T * k:T * (k + 1)],
                                start=(k == 0), stop=False),
                            waits=w if k == 0 else [], inc="td")
            if m == MC - 1:
                x16_free[tb] = t_mm
            while len(pend_up) > 1:
                emit_up(*pend_up.pop(0))
            t_mm = P.op("tensor",
                        lambda e, m=m, bk=bk: e.matmul(
                            pb[bk][:, :], bfcT16[:, 128 * m:128 * (m + 1)],
                            ufc16[:, :], start=False, stop=True),
                        waits=[t_ufc16[t], t_gsm], inc="td")
            bankD[bk] = t_mm
            t_g = P.op("scalar",
                       lambda e, m=m, bk=bk, t=t: e.activation(
                           out=g1[:, T * m:T * (m + 1)], in_=pb[bk][:, :],
                           func=AF.Gelu, bias=bfc_sb[:, m:m + 1],
                           scale=scl_sb[:, 1:2]),
                       waits=[t_mm, t_small], inc="sd")
            bankF[bk] = t_g
            gelu_done[(t, m)] = t_g
            pend_up.append((m, t_g))
        if c4 == 3:
            while pend_up:
                emit_up(*pend_up.pop(0))

    def fc(t, t_xn):
        fc_upre(t, t_xn)
        for c4 in range(4):
            fc_group(t, t_xn, c4, fc_group_tok[3])

    if KSTOP <= 2:
        return finalize()
    if DBG:
        dma("gpsimd", dbg_xn[:, :], x16[0][:, :], "gdbg", waits=[t_xn0])
    fc_upre(0, t_xn0)
    drain_q(24)
    for c4 in range(4):
        while fc_group_tok[c4] is None:
            drain_q(1)
        fc_group(0, t_xn0, c4, fc_group_tok[c4])
        drain_q(11)
    drain_q(len(qplan))
    if DBG:
        dma("gpsimd", dbg_g1[:, :], g1[:, 0:8 * T], "gdbg",
            waits=[gelu_done[(0, 7)], upr_free[0]])
        dma("gpsimd", dbg_u[:, 0:T], ufc16[:, :], "gdbg", waits=[t_ufc16[0]])
        dma("gpsimd", dbg_u[:, T:2 * T], upr16[:, :], "gdbg", waits=[t_upr16[0]])
        t_dbg = ("sd", P.semcount["sd"])
        t_dbv = ("vd", P.semcount["vd"])
        dma("gpsimd", dbg_scl[:, :], scl_sb[:, :], "gdbg", waits=[t_dbg, t_dbv])
        dma("gpsimd", dbg_w0[:, :], weff_fc[0][:, :], "gdbg", waits=[t_dbg])
        dma("gpsimd", dbg_w5[:, :], weff_fc[5][:, :], "gdbg", waits=[t_dbg])
        dma("gpsimd", dbg_wp[:, :], wpr_eff[0:256, :], "gdbg", waits=[("sw0", P.semcount["sw0"]), ("sw1", P.semcount["sw1"])])
        dma("gpsimd", dbg_ats[:, 0:512], aprTs[:, :], "gdbg", waits=[t_dbv])
        dma("gpsimd", dbg_ats[:, 512:640], afcTs[:, :], "gdbg", waits=[t_dbv])

    if KSTOP <= 3:
        return finalize()

    if KSTOP <= 4:
        return finalize()
    t_fl = dma("gpsimd", flat_sb[0:1, :], ag_out.ap().rearrange("j c -> (j c)"),
               "gsc", waits=[t_ag])
    t_hi = P.op("vector", lambda e: e.memset(hin_row[:, :], 0.0), inc="vd")
    for j in range(8):
        As = flat_sb[0:1, 64 * j:64 * j + 32]
        Us = flat_sb[0:1, 64 * j + 32:64 * (j + 1)]
        sj = sel_sb[0:1, j:j + 1]
        t_t = P.op("vector", lambda e, As=As, sj=sj: e.tensor_scalar(
            tmp_row[0:1, 0:32], As, -1.0, sj, OP.add, OP.mult),
            waits=[t_fl, t_small, t_hi], inc="vd")
        t_u = P.op("vector", lambda e, Us=Us, sj=sj: e.tensor_single_scalar(
            tmp_row[0:1, 32:64], Us, sj, OP.mult),
            waits=[t_fl], inc="vd")
        t_m = P.op("vector", lambda e: e.tensor_mul(tmp_row[0:1, 0:32], tmp_row[0:1, 0:32],
                                                    hin_row[0:1, :]),
                   waits=[t_t], inc="vd")
        t_a1 = P.op("vector", lambda e: e.tensor_add(hin_row[0:1, :], hin_row[0:1, :],
                                                     tmp_row[0:1, 0:32]),
                    waits=[t_m], inc="vd")
        t_hi = P.op("vector", lambda e: e.tensor_add(hin_row[0:1, :], hin_row[0:1, :],
                                                     tmp_row[0:1, 32:64]),
                    waits=[t_a1, t_u], inc="vd")
    t_hw = dma("gpsimd", hin_d[0:1, :], hin_row[0:1, :], "gsc", waits=[t_hi])
    t_hr = dma("gpsimd", hin_col.ap(), hin_d[0:1, :].rearrange("o f -> f o"), "gsc",
               waits=[t_hw], slow=True)
    t_fix = P.op("vector", lambda e: e.scalar_tensor_tensor(
        scanB[:, :], scanA[:, :], hin_col[:, :], scanB[:, :], OP.mult, OP.add),
        waits=[t_hr], inc="vd")
    t_h16 = P.op("vector", lambda e: e.tensor_copy(h16[0:32, :], scanB[:, :]),
                 waits=[t_fix], inc="vd")
    if DBG:
        dma("gpsimd", dbg_h16[:, :], h16[:, :], "gdbg", waits=[t_h16, t_h16one])
        dma("gpsimd", dbg_sA[:, :], scanA[:, :], "gdbg", waits=[t_fix])

    wprs_tok = {}
    outst_free = [None, None]
    xfin_free = [a_toks[-1][0], None]
    t_store_last = None

    def projwoutfin(t):
        nonlocal t_store_last
        tb = t % 2
        for k in range(PKC):
            wb = 2 + (k % 4)
            if t == 0 and k < 4:
                prev_use = None
            elif t > 0 and k < 4:
                prev_use = wprs_tok.get((t - 1, PKC - 4 + k))
            else:
                prev_use = wprs_tok.get((t, k - 4))
            t_ldp = dma("sync", wprs[wb][:, :],
                        wpr_eff[128 * k:128 * (k + 1), :], f"pl{k % 4}",
                        waits=[spill_tok[k], prev_use])
            for m in range(PMC):
                w = [t_ldp, gelu_done[(t, k)]]
                if k == 0:
                    w.append(bankF[m] or bankD[m])
                t_mm = P.op("tensor",
                            lambda e, k=k, m=m, wb=wb: e.matmul(
                                pb[m][:, :], wprs[wb][:, 128 * m:128 * (m + 1)],
                                g1[:, T * k:T * (k + 1)],
                                start=(k == 0), stop=False),
                            waits=w, inc="td")
                if m == PMC - 1:
                    wprs_tok[(t, k)] = t_mm
        for m in range(PMC):
            t_lo = P.op("tensor",
                        lambda e, m=m: e.matmul(
                            pb[m][:, :], bprT16[:, 128 * m:128 * (m + 1)],
                            upr16[:, :], start=False, stop=True),
                        waits=[t_upr16[t], t_gsm], inc="td")
            bankD[m] = t_lo
        fin_toks = finals_of[t] = []
        th_free = None
        for m in range(PMC):
            gb = m % 2
            t_th = P.op("scalar",
                        lambda e, m=m: e.activation(out=th_s[:, :], in_=pb[m][:, :],
                                                    func=AF.Tanh,
                                                    scale=scl_sb[:, 4:5],
                                                    bias=bprh_sb[:, m:m + 1]),
                        waits=[bankD[m], t_bprh, t_sclh, th_free], inc="sd")
            t_p1 = P.op("scalar",
                        lambda e, m=m: e.activation(out=p1_s[:, :], in_=pb[m][:, :],
                                                    func=AF.Identity,
                                                    scale=scl_sb[:, 3:4],
                                                    bias=bpr_sb[:, m:m + 1]),
                        waits=[t_th], inc="sd")
            t_g2 = P.op("vector",
                        lambda e, gb=gb: e.scalar_tensor_tensor(
                            g_s[gb][:, :], th_s[:, :], 1.0, p1_s[:, :], OP.add, OP.mult),
                        waits=[t_p1], inc="vd")
            th_free = t_g2
            t_wo = P.op("tensor",
                        lambda e, m=m, t=t: e.matmul(pb[m][:, :],
                                                     woutb16[:, 128 * m:128 * (m + 1)],
                                                     h16[:, T * t:T * (t + 1)],
                                                     start=True, stop=True),
                        waits=[t_p1, t_th, t_h16, t_h16one, t_gsm], inc="td")
            bankD[m] = t_wo
            xb = m % 2
            t_xf = dma("scalar", xfin[xb][:, :],
                       xT[:, T * t:T * (t + 1)].rearrange("(k p) f -> k p f", p=128)[m],
                       f"xf{xb}", waits=[xfin_free[xb]])
            ob = m % 2
            t_f3 = P.op("vector",
                        lambda e, ob=ob, m=m, tb=tb: e.tensor_add(
                            outst[ob][:, :], x16[tb][:, T * m:T * (m + 1)], pb[m][:, :]),
                        waits=[t_wo, t_g2, outst_free[ob]], inc="vd")
            bankF[m] = t_f3
            t_f4 = P.op("vector",
                        lambda e, ob=ob, gb=gb: e.tensor_mul(outst[ob][:, :], outst[ob][:, :],
                                                             g_s[gb][:, :]),
                        inc="vd")
            t_f5 = P.op("vector",
                        lambda e, ob=ob, xb=xb: e.scalar_tensor_tensor(
                            outst[ob][:, :], outst[ob][:, :], 0.5, xfin[xb][:, :],
                            OP.mult, OP.add),
                        waits=[t_xf], inc="vd")
            xfin_free[xb] = t_f5
            fin_toks.append(t_f5)
            t_st = dma("gpsimd",
                       outT[:, T * t:T * (t + 1)].rearrange("(k p) f -> k p f", p=128)[m],
                       outst[ob][:, :], f"st{ob}", waits=[t_f5])
            outst_free[ob] = t_st
            t_store_last = t_st

    if KSTOP <= 5:
        return finalize()
    for t in range(NT):
        if t + 1 < NT:
            t_xn_next = prep(t + 1)
        projwoutfin(t)
        if t + 1 < NT:
            fc(t + 1, t_xn_next)
        if KSTOP <= 6 + t:
            return finalize()
    if KSTOP <= 9:
        return finalize()

    P.op("gpsimd", lambda e: None,
         waits=[("st0", P.semcount["st0"]), ("st1", P.semcount["st1"])], inc="done")

    with nc.Block() as block:
        P.emit_all(block)
    P._es.close()
    es.close()
    return nc


def make_inputs(full):
    x = np.asarray(full["L0_attn_x"], np.float32)
    W_a = np.asarray(full["L0_attn_W_a"], np.float32)
    b_a = np.asarray(full["L0_attn_b_a"], np.float32)
    W_in = np.asarray(full["L0_attn_W_in"], np.float32)
    b_in = np.asarray(full["L0_attn_b_in"], np.float32)
    W_out = np.asarray(full["L0_attn_W_out"], np.float32)
    b_out = np.asarray(full["L0_attn_b_out"], np.float32)
    gamma = np.asarray(full["L0_attn_ln_gamma"], np.float32)
    beta = np.asarray(full["L0_attn_ln_beta"], np.float32)
    W_fc = np.asarray(full["L0_ffn_W_fc"], np.float32)
    b_fc = np.asarray(full["L0_ffn_b_fc"], np.float32)
    A_fc = np.asarray(full["L0_ffn_A_fc"], np.float32)
    B_fc = np.asarray(full["L0_ffn_B_fc"], np.float32)
    W_pr = np.asarray(full["L0_ffn_W_proj"], np.float32)
    b_pr = np.asarray(full["L0_ffn_b_proj"], np.float32)
    A_pr = np.asarray(full["L0_ffn_A_proj"], np.float32)
    B_pr = np.asarray(full["L0_ffn_B_proj"], np.float32)

    C = np.ascontiguousarray
    shared = dict(
        wfcT=C(W_fc.T), wprT=C(W_pr.T), waT=C(W_a.T), wiT=C(W_in.T),
        woutbT=C(np.vstack([W_out.T, b_out[None, :]])),
        afcTd=C(A_fc.T.reshape(KC, 128, 16).transpose(1, 0, 2).reshape(128, 128)),
        bfcTd=C(B_fc.T),
        aprTd=C(A_pr.T.reshape(PKC, 128, 16).transpose(1, 0, 2).reshape(128, 512)),
        bprTd=C(B_pr.T),
        gamma8=C(gamma.reshape(KC, 128).T), beta8=C(beta.reshape(KC, 128).T),
        bfc32d=C(b_fc.reshape(MC, 128).T), bpr8d=C(b_pr.reshape(PMC, 128).T),
        babi2=C(np.stack([b_a, b_in], axis=1)),
    )
    wfcT_l = shared["wfcT"]
    wprT_l = shared["wprT"]
    maps = []
    for c in range(8):
        b, q = c // 4, c % 4
        sel = np.zeros((1, 8), np.float32)
        for j in range(8):
            if j // 4 == b and j % 4 < q:
                sel[0, j] = 1.0
        m = dict(shared)
        m.update(
            xT=C(x[b, TOK * q:TOK * (q + 1), :].T),
            sel8=sel,
            shard_fc=C(wfcT_l[128 * c:128 * (c + 1), :]),
            shard_pr=C(wprT_l[512 * c:512 * (c + 1), :]),
        )
        maps.append(m)
    return maps


def assemble(results):
    out = np.empty((2, 8192, D), np.float32)
    for c, r in enumerate(results):
        b, q = c // 4, c % 4
        out[b, TOK * q:TOK * (q + 1), :] = np.asarray(r["outT"]).T
    return out


from concourse.bass_utils import run_bass_kernel_spmd


def kernel(**inputs):
    nc = build_nc()
    maps = make_inputs(inputs)
    res = run_bass_kernel_spmd(nc, maps, core_ids=list(range(8)))
    return assemble(res.results)
